# version 06
# baseline (speedup 1.0000x reference)
import os
import numpy as np
import concourse.bass as bass
import concourse.mybir as mybir
from concourse.bass_utils import run_bass_kernel_spmd
from contextlib import ExitStack

F32 = mybir.dt.float32
BF16 = mybir.dt.bfloat16
FP16 = mybir.dt.float16
AF = mybir.ActivationFunctionType
ALU = mybir.AluOpType
AX = mybir.AxisListType

D = 2048
H = 4096
TP = 256
SEQ = 2048
NT = 2 * SEQ
NCH = NT // TP
KD = D // 128
KH = H // 128
EPS = 1e-5
NSA = 3
NSB = 4

ENGS = ("tensor", "vector", "scalar", "gpsimd", "sync")


class Prog:
    def __init__(self, nc, es):
        self.nc = nc
        self.streams = {e: [] for e in ENGS}
        self.sems = {}
        self.cnt = {}
        for name in ("wd", "gd", "od", "tn", "vc", "sc"):
            self.sems[name] = es.enter_context(nc.semaphore(name))
            self.cnt[name] = 0

    def wait(self, eng, sem, val):
        if val <= 0:
            return
        s = self.sems[sem]
        self.streams[eng].append(lambda e, s=s, v=val: e.wait_ge(s, v))

    def waits(self, eng, lst):
        for sem, val in lst:
            self.wait(eng, sem, val)

    def go(self, eng, fn, inc=None, n=1):
        if inc is None:
            self.streams[eng].append(fn)
            return None
        self.cnt[inc] += n
        s = self.sems[inc]

        def run(e, fn=fn, s=s, n=n):
            fn(e).then_inc(s, n)
        self.streams[eng].append(run)
        return self.cnt[inc]


def build(nch=NCH):
    nc = bass.Bass("TRN2", target_bir_lowering=False, debug=False,
                   detect_race_conditions=False)
    x_d = nc.dram_tensor("x", [NT, D], F32, kind="ExternalInput").ap()
    y_d = nc.dram_tensor("y", [NT, D], F32, kind="ExternalOutput").ap()
    wit_d = nc.dram_tensor("wit", [KD, 128, D], F32, kind="ExternalInput").ap()
    wft_d = nc.dram_tensor("wft", [KD, 128, D], F32, kind="ExternalInput").ap()
    wgt_d = nc.dram_tensor("wgt", [KD, 128, D], F32, kind="ExternalInput").ap()
    wot_d = nc.dram_tensor("wot", [KD, 128, D], F32, kind="ExternalInput").ap()
    w1t_d = nc.dram_tensor("w1t", [KH, 128, D], F32, kind="ExternalInput").ap()
    w2t_d = nc.dram_tensor("w2t", [KH, 128, H], F32, kind="ExternalInput").ap()
    w3t_d = nc.dram_tensor("w3t", [KD, 128, H], F32, kind="ExternalInput").ap()
    svi_d = nc.dram_tensor("svi", [128, KD], F32, kind="ExternalInput").ap()
    svf_d = nc.dram_tensor("svf", [128, KD], F32, kind="ExternalInput").ap()
    svg_d = nc.dram_tensor("svg", [128, KD], F32, kind="ExternalInput").ap()
    svo_d = nc.dram_tensor("svo", [128, KD], F32, kind="ExternalInput").ap()
    gnw_d = nc.dram_tensor("gnw", [128, KD], F32, kind="ExternalInput").ap()
    b1_d = nc.dram_tensor("b1m", [128, KH], F32, kind="ExternalInput").ap()
    b2_d = nc.dram_tensor("b2m", [128, KH], F32, kind="ExternalInput").ap()
    b3_d = nc.dram_tensor("b3m", [128, KD], F32, kind="ExternalInput").ap()
    id_d = nc.dram_tensor("ident", [128, 128], F32, kind="ExternalInput").ap()
    onc_d = nc.dram_tensor("onec", [128, 1], F32, kind="ExternalInput").ap()
    onr_d = nc.dram_tensor("oner", [1, 128], F32, kind="ExternalInput").ap()
    wis_d = nc.dram_tensor("wis", [KD, 128, D], FP16).ap()
    wfs_d = nc.dram_tensor("wfs", [KD, 128, D], FP16).ap()
    wgs_d = nc.dram_tensor("wgs", [KD, 128, D], FP16).ap()
    wos_d = nc.dram_tensor("wos", [KD, 128, D], BF16).ap()
    w1h_d = nc.dram_tensor("w1h", [KH, 128, D], BF16).ap()
    w1l_d = nc.dram_tensor("w1l", [KH, 128, D], BF16).ap()
    w2h_d = nc.dram_tensor("w2h", [KH, 128, H], BF16).ap()
    w2l_d = nc.dram_tensor("w2l", [KH, 128, H], BF16).ap()
    w3h_d = nc.dram_tensor("w3h", [KD, 128, H], BF16).ap()
    w3l_d = nc.dram_tensor("w3l", [KD, 128, H], BF16).ap()

    es = ExitStack()
    with es:
        sb = lambda name, shape, dt=F32: es.enter_context(
            nc.sbuf_tensor(name, shape, dt))
        xna = sb("xna", [128, D])
        xnb = sb("xnb", [128, D])
        xt = sb("xt", [128, KD * TP])
        zt = sb("zt", [128, KD * TP])
        lx = sb("lx", [128, KD * TP], FP16)
        lnh = sb("lnh", [128, KD * TP], BF16)
        lnl = sb("lnl", [128, KD * TP], BF16)
        hb = sb("hb", [128, KD * TP])
        gb = sb("gb", [128, KD * TP])
        h1h = sb("h1h", [128, KH * TP], BF16)
        h1l = sb("h1l", [128, KH * TP], BF16)
        h2h = sb("h2h", [128, KH * TP], BF16)
        h2l = sb("h2l", [128, KH * TP], BF16)
        wpa = [sb(f"wpa{j}", [128, D], FP16) for j in range(NSA)]
        wpb = [sb(f"wpb{j}", [128, D], BF16) for j in range(NSB)]
        sv_i = sb("sv_i", [128, KD])
        sv_f = sb("sv_f", [128, KD])
        sv_g = sb("sv_g", [128, KD])
        sv_o = sb("sv_o", [128, KD])
        sv_gn = sb("sv_gn", [128, KD])
        b1s = sb("b1s", [128, KH])
        b2s = sb("b2s", [128, KH])
        b3s = sb("b3s", [128, KD])
        id32 = sb("id32", [128, 128])
        onc = sb("onc", [128, 1])
        onr = sb("onr", [1, 128])
        cst = sb("cst", [128, 2])
        svr = sb("svr", [128, 16])
        rowb = sb("rowb", [1, 768])
        bc = sb("bc", [128, 512])
        t1 = [sb(f"t1{j}", [128, TP]) for j in range(2)]
        gsg = t1
        carry = sb("carry", [128, KD])
        psb = lambda name: es.enter_context(
            nc.psum_tensor(name, [128, 512], F32))
        psh = lambda name: es.enter_context(
            nc.psum_tensor(name, [128, 256], F32))
        ps_i = psh("ps_i")
        ps_f = psh("ps_f")
        ps_g = psh("ps_g")
        ps_mm = [psh("ps_mm0"), psh("ps_mm1")]
        ps_trA = psb("ps_trA")
        ps_trB = psb("ps_trB")
        ps_sb = psb("ps_sb")
        tr_banks = [ps_trA, ps_trB]

        p = Prog(nc, es)

        for dst, src in ((sv_i, svi_d), (sv_f, svf_d), (sv_g, svg_d),
                         (sv_o, svo_d), (sv_gn, gnw_d), (b1s, b1_d),
                         (b2s, b2_d), (b3s, b3_d), (id32, id_d),
                         (onc, onc_d), (onr, onr_d)):
            p.go("sync", lambda e, d=dst, s=src: e.dma_start(out=d[:], in_=s),
                 inc="wd", n=16)
        wd_small = p.cnt["wd"]
        p.wait("vector", "wd", wd_small)
        p.go("vector", lambda e: e.memset(cst[:, 0:1], 4096.5), inc="vc")
        p.go("vector", lambda e: e.memset(cst[:, 1:2], EPS), inc="vc")
        p.go("vector", lambda e: e.tensor_scalar(
            sv_i[:], sv_i[:], 1.0 / 256.0, None, ALU.mult), inc="vc")
        p.go("vector", lambda e: e.tensor_scalar(
            sv_f[:], sv_f[:], 1.0 / 256.0, None, ALU.mult), inc="vc")
        p.go("vector", lambda e: e.tensor_scalar(
            sv_g[:], sv_g[:], 1.0 / 256.0, None, ALU.mult), inc="vc")
        p.go("vector", lambda e: e.tensor_mul(sv_gn[:], sv_gn[:], sv_o[:]),
             inc="vc")
        p.go("vector", lambda e: e.drain())

        pbuf = [xna, xnb]
        pbuf_rd = [0, 0]
        slot_rd = {}

        def prep_cast(src_t, n_slab, width, hi_scr, lo_scr, hi_slot, lo_slot):
            for s_ in range(n_slab):
                for half in range(width // D):
                    j = (s_ * (width // D) + half) % 2
                    p.wait("sync", "vc", pbuf_rd[j])
                    ld = p.go("sync", lambda e, j=j, s_=s_, half=half:
                              e.dma_start(
                                  out=pbuf[j][:],
                                  in_=src_t[s_, :, half * D:(half + 1) * D]),
                              inc="wd", n=16)
                    p.wait("vector", "wd", ld)
                    p.wait("vector", "wd", slot_rd.get(id(hi_slot), 0))
                    cv = p.go("vector", lambda e, j=j, sl=hi_slot:
                              e.tensor_copy(sl[:], pbuf[j][:]), inc="vc")
                    if lo_scr is not None:
                        p.wait("vector", "wd", slot_rd.get(id(lo_slot), 0))
                        cv = p.go("vector", lambda e, j=j, hsl=hi_slot,
                                  lsl=lo_slot:
                                  e.tensor_sub(lsl[:], pbuf[j][:], hsl[:]),
                                  inc="vc")
                    pbuf_rd[j] = cv
                    p.wait("sync", "vc", cv)
                    for sl, scr in ([(hi_slot, hi_scr)] +
                                    ([(lo_slot, lo_scr)] if lo_scr is not None
                                     else [])):
                        slot_rd[id(sl)] = p.go(
                            "sync", lambda e, sl=sl, scr=scr, s_=s_, half=half:
                            e.dma_start(
                                out=scr[s_, :, half * D:(half + 1) * D],
                                in_=sl[:]), inc="wd", n=16)

        prep_cast(wit_d, KD, D, wis_d, None, wpa[0], None)
        prep_cast(wft_d, KD, D, wfs_d, None, wpa[1], None)
        prep_cast(wgt_d, KD, D, wgs_d, None, wpa[2], None)
        prep_cast(wot_d, KD, D, wos_d, None, wpb[0], None)
        prep_cast(w1t_d, KH, D, w1h_d, w1l_d, wpb[1], wpb[2])
        prep_cast(w2t_d, KH, H, w2h_d, w2l_d, wpb[3], wpb[0])
        prep_cast(w3t_d, KD, H, w3h_d, w3l_d, wpb[3], wpb[1])

        prep_end = {s: p.cnt[s] for s in ("wd", "vc", "tn", "sc")}
        init_xn_free = [("vc", prep_end["vc"])]

        class Ring:
            def __init__(self, slots, init_waits):
                self.slots = slots
                self.n = len(slots)
                self.load_idx = 0
                self.freed = []
                self.init_waits = init_waits

            def submit(self, srcs):
                out = []
                for src in srcs:
                    slot = self.slots[self.load_idx % self.n]
                    if self.load_idx >= self.n:
                        p.waits("sync", self.freed[self.load_idx - self.n])
                    else:
                        p.waits("sync", self.init_waits)
                    p.go("sync", lambda e, d=slot, s=src:
                         e.dma_start(out=d[:], in_=s), inc="wd", n=16)
                    out.append(slot)
                    self.load_idx += 1
                return out, p.cnt["wd"]

            def consumed(self, n, waits):
                for _ in range(n):
                    self.freed.append(list(waits))

        ringA = Ring(wpa, [("wd", prep_end["wd"])])
        ringB = Ring(wpb, [("wd", prep_end["wd"])])

        bfree = {"i": [], "f": [], "g": [], "mm0": [], "mm1": [],
                 "trA": [], "trB": [], "sb": []}

        gate = {
            "xn_free": list(init_xn_free),
            "xt_free": [],
            "lx_free": [],
            "lnf_free": [],
            "hb_free": [],
            "gb_free": [],
            "zt_free": [],
            "h1_free": [],
            "h2_free": [],
        }

        gsg_free = [0, 0]

        def transpose_pass(src_tiles, dst_fn, waits_tensor, waits_vector):
            n = len(src_tiles)
            last_ev = None
            for r in range((n + 3) // 4):
                bank_i = r % 2
                bank = tr_banks[bank_i]
                key = "trA" if bank_i == 0 else "trB"
                p.waits("tensor", bfree[key])
                if r == 0:
                    p.waits("tensor", waits_tensor)
                tt = None
                for j in range(4):
                    idx = r * 4 + j
                    if idx >= n:
                        break
                    tt = p.go("tensor", lambda e, b=bank, j=j, s=src_tiles[idx]:
                              e.transpose(b[:, j * 128:(j + 1) * 128], s,
                                          id32[:]), inc="tn")
                p.wait("vector", "tn", tt)
                if r == 0:
                    p.waits("vector", waits_vector)
                ev = None
                for j in range(4):
                    idx = r * 4 + j
                    if idx >= n:
                        break
                    ev = p.go("vector", lambda e, b=bank, j=j, d=dst_fn(idx):
                              e.tensor_copy(d, b[:, j * 128:(j + 1) * 128]),
                              inc="vc")
                bfree[key] = [("vc", ev)]
                last_ev = ev
            return last_ev

        for c in range(nch):
            r0 = c * TP

            p.waits("gpsimd", gate["xn_free"])
            gd_x = None
            for i in range(2):
                gd_x = p.go("gpsimd", lambda e, i=i, r0=r0:
                            e.dma_start(out=(xna if i == 0 else xnb)[:],
                                        in_=x_d[r0 + i * 128:r0 + (i + 1) * 128, :]),
                            inc="gd", n=16)

            xtiles = [(xna if i == 0 else xnb)[:, kd * 128:(kd + 1) * 128]
                      for i in range(2) for kd in range(KD)]
            dstx = lambda idx: xt[:, (idx % KD) * TP + (idx // KD) * 128:
                                  (idx % KD) * TP + (idx // KD) * 128 + 128]
            p.wait("vector", "gd", gd_x)
            p.waits("vector", gate["xt_free"])
            transpose_pass(xtiles, dstx, [("gd", gd_x)] + gate["xt_free"], [])
            tn_xtr = p.cnt["tn"]
            for i in range(2):
                src = xna if i == 0 else xnb
                p.go("vector", lambda e, s=src, i=i:
                     e.reduce_sum(svr[:, i:i + 1], s[:], axis=AX.X), inc="vc")
            p.waits("scalar", [("gd", gd_x)] + gate["zt_free"])
            for i in range(2):
                src = xna if i == 0 else xnb
                sc_sq = p.go("scalar", lambda e, s=src, i=i:
                             e.activation(zt[:, i * D:(i + 1) * D], s[:],
                                          AF.Square), inc="sc")
            p.wait("vector", "sc", sc_sq)
            for i in range(2):
                p.go("vector", lambda e, i=i:
                     e.reduce_sum(svr[:, 2 + i:3 + i], zt[:, i * D:(i + 1) * D],
                                  axis=AX.X), inc="vc")
            p.go("vector", lambda e: e.drain())
            for i in range(2):
                p.go("vector", lambda e, i=i: e.tensor_scalar(
                    svr[:, 4 + i:5 + i], svr[:, i:i + 1], 1.0 / D, None,
                    ALU.mult), inc="vc")
            p.go("vector", lambda e: e.drain())
            for i in range(2):
                p.go("vector", lambda e, i=i: e.tensor_scalar(
                    svr[:, 6 + i:7 + i], svr[:, 2 + i:3 + i], 1.0 / D, None,
                    ALU.mult), inc="vc")
            p.go("vector", lambda e: e.drain())
            for i in range(2):
                p.go("vector", lambda e, i=i: e.tensor_mul(
                    svr[:, 8 + i:9 + i], svr[:, 4 + i:5 + i],
                    svr[:, 4 + i:5 + i]), inc="vc")
            p.go("vector", lambda e: e.drain())
            for i in range(2):
                vc_var = p.go("vector", lambda e, i=i: e.tensor_sub(
                    svr[:, 6 + i:7 + i], svr[:, 6 + i:7 + i],
                    svr[:, 8 + i:9 + i]), inc="vc")
            p.go("vector", lambda e: e.drain())
            p.wait("scalar", "vc", vc_var)
            for i in range(2):
                sc_sd = p.go("scalar", lambda e, i=i: e.activation(
                    svr[:, 8 + i:9 + i], svr[:, 6 + i:7 + i], AF.Sqrt,
                    bias=cst[:, 1:2], scale=1.0), inc="sc")
            p.wait("vector", "sc", sc_sd)
            for i in range(2):
                p.go("vector", lambda e, i=i: e.reciprocal(
                    svr[:, 10 + i:11 + i], svr[:, 8 + i:9 + i]), inc="vc")
            p.go("vector", lambda e: e.drain())
            for i in range(2):
                p.go("vector", lambda e, i=i: e.tensor_scalar(
                    svr[:, 12 + i:13 + i], svr[:, 10 + i:11 + i], 256.0, None,
                    ALU.mult), inc="vc")
            p.go("vector", lambda e: e.drain())
            for i in range(2):
                p.go("vector", lambda e, i=i: e.tensor_mul(
                    svr[:, 10 + i:11 + i], svr[:, 12 + i:13 + i],
                    svr[:, 4 + i:5 + i]), inc="vc")
            p.go("vector", lambda e: e.drain())
            for i in range(2):
                vc_rows = p.go("vector", lambda e, i=i: e.tensor_scalar(
                    svr[:, 14 + i:15 + i], svr[:, 10 + i:11 + i], -1.0, 2048.5,
                    ALU.mult, ALU.add), inc="vc")
            p.go("vector", lambda e: e.drain())
            p.wait("scalar", "vc", vc_rows)
            p.wait("scalar", "tn", tn_xtr)
            sc_ap = None
            for i in range(2):
                src = xna if i == 0 else xnb
                sc_ap = p.go("scalar", lambda e, s=src, i=i: e.activation(
                    s[:], s[:], AF.Identity, bias=svr[:, 14 + i:15 + i],
                    scale=svr[:, 12 + i:13 + i]), inc="sc")
            ztiles = [(xna if i == 0 else xnb)[:, kd * 128:(kd + 1) * 128]
                      for i in range(2) for kd in range(KD)]
            dstz = lambda idx: zt[:, (idx % KD) * TP + (idx // KD) * 128:
                                  (idx % KD) * TP + (idx // KD) * 128 + 128]
            transpose_pass(ztiles, dstz,
                           [("sc", sc_ap), ("vc", p.cnt["vc"])], [])
            p.waits("vector", [("tn", p.cnt["tn"])] + gate["lx_free"])
            for hh in range(2):
                zh = zt[:, hh * D:(hh + 1) * D]
                d_ = xna
                c_ = xnb
                p.go("vector", lambda e, zh=zh: e.tensor_copy(d_[:], zh),
                     inc="vc")
                for b in range(11, -1, -1):
                    p.go("vector", lambda e, b=b: e.tensor_scalar(
                        c_[:], d_[:], float(2 ** b), None, ALU.is_ge),
                        inc="vc")
                    p.go("vector", lambda e, b=b: e.scalar_tensor_tensor(
                        d_[:], c_[:], -float(2 ** b), d_[:], ALU.mult,
                        ALU.add), inc="vc")
                p.go("vector", lambda e, zh=zh: e.scalar_tensor_tensor(
                    zh, d_[:], -1.0, zh, ALU.mult, ALU.add), inc="vc")
                vc_lnf = p.go("vector", lambda e, zh=zh, hh=hh: e.tensor_scalar(
                    lx[:, hh * D:(hh + 1) * D], zh, -2048.0, None, ALU.add),
                    inc="vc")
            first_ifg = True
            sc_f_done = vc_ifg_done = None
            ATTNMONO = True
            for s_ in range(KD):
                tn_w = {}
                if ATTNMONO:
                    slots3, wdn3 = ringA.submit([wis_d[s_, :, :],
                                                 wfs_d[s_, :, :],
                                                 wgs_d[s_, :, :]])
                for wj, (scr, psap, key) in enumerate(
                        ((wis_d, ps_i, "i"),
                         (wfs_d, ps_f, "f"),
                         (wgs_d, ps_g, "g"))):
                    if ATTNMONO:
                        slot, wdn = slots3[wj], wdn3
                    else:
                        slots, wdn = ringA.submit([scr[s_, :, :]])
                        slot = slots[0]
                    p.wait("tensor", "wd", wdn)
                    if first_ifg:
                        p.wait("tensor", "vc", vc_lnf)
                        first_ifg = False
                    p.waits("tensor", bfree[key])
                    for kd in range(KD):
                        p.go("tensor", lambda e, psap=psap, slot=slot, kd=kd,
                             st=(kd == 0), sp=(kd == KD - 1):
                             e.matmul(psap[:],
                                      slot[:, kd * 128:(kd + 1) * 128],
                                      lx[:, kd * TP:kd * TP + TP],
                                      start=st, stop=sp),
                             inc="tn" if kd == KD - 1 else None)
                    tn_w[key] = p.cnt["tn"]
                    if not ATTNMONO:
                        ringA.consumed(1, [("tn", p.cnt["tn"])])
                if ATTNMONO:
                    ringA.consumed(3, [("tn", p.cnt["tn"])])
                p.wait("vector", "tn", tn_w["i"])
                if s_ == 0:
                    p.waits("vector", gate["hb_free"] + gate["gb_free"])
                e1 = p.go("vector", lambda e, s_=s_: e.tensor_scalar(
                    hb[:, s_ * TP:(s_ + 1) * TP], ps_i[:],
                    sv_i[:, s_:s_ + 1], None, ALU.mult), inc="vc")
                bfree["i"] = [("vc", e1)]
                p.wait("scalar", "tn", tn_w["f"])
                if s_ == 0:
                    p.waits("scalar", [("vc", vc_lnf)])
                e2 = p.go("scalar", lambda e, s_=s_: e.activation(
                    zt[:, s_ * TP:(s_ + 1) * TP], ps_f[:], AF.Sigmoid,
                    scale=sv_f[:, s_:s_ + 1]), inc="sc")
                bfree["f"] = [("sc", e2)]
                par = s_ % 2
                p.wait("vector", "tn", tn_w["g"])
                e4 = p.go("vector", lambda e, s_=s_: e.tensor_scalar(
                    gb[:, s_ * TP:(s_ + 1) * TP], ps_g[:],
                    sv_g[:, s_:s_ + 1], None, ALU.mult), inc="vc")
                bfree["g"] = [("vc", e4)]
                p.wait("scalar", "vc", e4)
                p.wait("scalar", "vc", gsg_free[par])
                e3 = p.go("scalar", lambda e, par=par, s_=s_: e.activation(
                    gsg[par][:], gb[:, s_ * TP:(s_ + 1) * TP], AF.Sigmoid),
                    inc="sc")
                p.wait("vector", "sc", e3)
                vc_ifg_done = p.go("vector", lambda e, s_=s_, par=par:
                                   e.tensor_mul(
                                       gb[:, s_ * TP:(s_ + 1) * TP],
                                       gb[:, s_ * TP:(s_ + 1) * TP],
                                       gsg[par][:]), inc="vc")
                gsg_free[par] = vc_ifg_done
                sc_f_done = e2
            p.wait("vector", "sc", sc_f_done)
            for hh in range(2):
                tmp = xna if hh == 0 else xnb
                p.go("vector", lambda e, hh=hh, tmp=tmp: e.tensor_mul(
                    tmp[:], zt[:, hh * D:(hh + 1) * D],
                    hb[:, hh * D:(hh + 1) * D]), inc="vc")
                p.go("vector", lambda e, hh=hh, tmp=tmp: e.tensor_sub(
                    hb[:, hh * D:(hh + 1) * D], hb[:, hh * D:(hh + 1) * D],
                    tmp[:]), inc="vc")
            for s_ in range(KD):
                init = (0.0 if c % (SEQ // TP) == 0 else carry[:, s_:s_ + 1])
                p.go("vector", lambda e, s_=s_, init=init:
                     e.tensor_tensor_scan(
                         hb[:, s_ * TP:(s_ + 1) * TP],
                         zt[:, s_ * TP:(s_ + 1) * TP],
                         hb[:, s_ * TP:(s_ + 1) * TP],
                         init, ALU.mult, ALU.add), inc="vc")
            p.go("vector", lambda e: e.drain())
            for s_ in range(KD):
                vc_carry = p.go("vector", lambda e, s_=s_: e.tensor_copy(
                    carry[:, s_:s_ + 1],
                    hb[:, s_ * TP + TP - 1:s_ * TP + TP]), inc="vc")
            p.wait("scalar", "vc", vc_carry)
            sq_sc = None
            for i in range(2):
                sq_sc = p.go("scalar", lambda e, i=i: e.activation(
                    (xna if i == 0 else xnb)[:], hb[:, i * D:(i + 1) * D],
                    AF.Square), inc="sc")
            p.wait("tensor", "sc", sq_sc)
            p.waits("tensor", bfree["sb"])
            for kd in range(KD):
                src = xna if kd < 8 else xnb
                tn_rms = p.go("tensor", lambda e, src=src, kd=kd: e.matmul(
                    ps_sb[0:1, 0:256], onc[:],
                    src[:, (kd % 8) * TP:(kd % 8 + 1) * TP],
                    start=(kd == 0), stop=(kd == KD - 1)),
                    inc="tn" if kd == KD - 1 else None)
            p.wait("scalar", "tn", tn_rms)
            sc_sd2 = p.go("scalar", lambda e: e.activation(
                rowb[0:1, 0:256], ps_sb[0:1, 0:256], AF.Sqrt,
                bias=cst[0:1, 1:2], scale=1.0 / D), inc="sc")
            p.wait("vector", "sc", sc_sd2)
            vc_rr = p.go("vector", lambda e: e.reciprocal(
                rowb[0:1, 256:512], rowb[0:1, 0:256]), inc="vc")
            p.wait("tensor", "vc", vc_rr)
            tn_bc = p.go("tensor", lambda e: e.matmul(
                ps_sb[:, 256:512], onr[0:1, :], rowb[0:1, 256:512],
                start=True, stop=True), inc="tn")
            p.wait("vector", "tn", tn_bc)
            vc_bc = p.go("vector", lambda e: e.tensor_copy(
                bc[:, 0:256], ps_sb[:, 256:512]), inc="vc")
            bfree["sb"] = [("vc", vc_bc)]
            p.wait("vector", "sc", sq_sc)
            for s_ in range(KD):
                p.go("vector", lambda e, s_=s_: e.tensor_mul(
                    hb[:, s_ * TP:(s_ + 1) * TP], hb[:, s_ * TP:(s_ + 1) * TP],
                    bc[:, 0:256]), inc="vc")
            p.go("vector", lambda e: e.tensor_mul(hb[:], hb[:], gb[:]),
                 inc="vc")
            for s_ in range(KD):
                p.go("vector", lambda e, s_=s_: e.tensor_scalar(
                    hb[:, s_ * TP:(s_ + 1) * TP], hb[:, s_ * TP:(s_ + 1) * TP],
                    sv_gn[:, s_:s_ + 1], None, ALU.mult), inc="vc")
            p.waits("vector", gate["lnf_free"])
            p.go("vector", lambda e: e.tensor_copy(lnh[:], hb[:]), inc="vc")
            vc_o = p.go("vector", lambda e: e.tensor_sub(lnl[:], hb[:], lnh[:]),
                        inc="vc")
            first_wo = True
            for os_ in range(KD):
                slots, wdn = ringB.submit([wos_d[os_, :, :]])
                par = os_ % 2
                wok = "mm0" if par == 0 else "mm1"
                psap = ps_mm[par]
                p.wait("tensor", "wd", wdn)
                if first_wo:
                    p.wait("tensor", "vc", vc_o)
                    first_wo = False
                p.waits("tensor", bfree[wok])
                nmm = 0
                for ke in range(KD):
                    for rhs in (lnh, lnl):
                        last = (ke == KD - 1 and rhs is lnl)
                        tn_wo = p.go("tensor", lambda e, psap=psap, ke=ke,
                                     rhs=rhs, st=(nmm == 0), sp=last,
                                     slot=slots[0]:
                                     e.matmul(psap[:],
                                              slot[:, ke * 128:(ke + 1) * 128],
                                              rhs[:, ke * TP:ke * TP + TP],
                                              start=st, stop=sp),
                                     inc="tn" if last else None)
                        nmm += 1
                ringB.consumed(1, [("tn", p.cnt["tn"])])
                p.wait("scalar", "tn", tn_wo)
                if os_ == 0:
                    p.wait("scalar", "vc", vc_carry)
                e4 = p.go("scalar", lambda e, os_=os_, psap=psap: e.activation(
                    zt[:, os_ * TP:(os_ + 1) * TP], psap[:], AF.Identity,
                    scale=256.0, bias=cst[:, 0:1]), inc="sc")
                bfree[wok] = [("sc", e4)]
            sc_wo = p.cnt["sc"]
            p.wait("vector", "sc", sc_wo)
            p.wait("vector", "tn", p.cnt["tn"])
            for hh in range(2):
                zh = zt[:, hh * D:(hh + 1) * D]
                d_ = xna
                c_ = xnb
                p.go("vector", lambda e, zh=zh: e.tensor_copy(d_[:], zh),
                     inc="vc")
                for b in range(12, -1, -1):
                    p.go("vector", lambda e, b=b: e.tensor_scalar(
                        c_[:], d_[:], float(2 ** b), None, ALU.is_ge), inc="vc")
                    p.go("vector", lambda e, b=b: e.scalar_tensor_tensor(
                        d_[:], c_[:], -float(2 ** b), d_[:], ALU.mult, ALU.add),
                        inc="vc")
                p.go("vector", lambda e, zh=zh: e.scalar_tensor_tensor(
                    zh, d_[:], -1.0, zh, ALU.mult, ALU.add), inc="vc")
                p.go("vector", lambda e, zh=zh: e.tensor_scalar(
                    zh, zh, 1.0 / 256.0, -16.0, ALU.mult, ALU.add), inc="vc")
                vc_y1 = p.go("vector", lambda e, zh=zh, hh=hh: e.tensor_add(
                    xt[:, hh * D:(hh + 1) * D], xt[:, hh * D:(hh + 1) * D],
                    zh), inc="vc")
            p.wait("scalar", "vc", vc_y1)
            sq2 = None
            for i in range(2):
                sq2 = p.go("scalar", lambda e, i=i: e.activation(
                    (xna if i == 0 else xnb)[:], xt[:, i * D:(i + 1) * D],
                    AF.Square), inc="sc")
            p.wait("tensor", "sc", sq2)
            p.waits("tensor", bfree["sb"])
            for kd in range(KD):
                p.go("tensor", lambda e, kd=kd: e.matmul(
                    ps_sb[0:1, 0:256], onc[:], xt[:, kd * TP:(kd + 1) * TP],
                    start=(kd == 0), stop=(kd == KD - 1)),
                    inc="tn" if kd == KD - 1 else None)
            for kd in range(KD):
                src = xna if kd < 8 else xnb
                tn_st2 = p.go("tensor", lambda e, src=src, kd=kd: e.matmul(
                    ps_sb[0:1, 256:512], onc[:],
                    src[:, (kd % 8) * TP:(kd % 8 + 1) * TP],
                    start=(kd == 0), stop=(kd == KD - 1)),
                    inc="tn" if kd == KD - 1 else None)
            p.wait("vector", "tn", tn_st2)
            p.go("vector", lambda e: e.tensor_scalar(
                rowb[0:1, 0:256], ps_sb[0:1, 0:256], 1.0 / D, None, ALU.mult),
                inc="vc")
            p.go("vector", lambda e: e.tensor_scalar(
                rowb[0:1, 256:512], ps_sb[0:1, 256:512], 1.0 / D, None,
                ALU.mult), inc="vc")
            p.go("vector", lambda e: e.drain())
            p.go("vector", lambda e: e.tensor_mul(
                rowb[0:1, 512:768], rowb[0:1, 0:256], rowb[0:1, 0:256]),
                inc="vc")
            p.go("vector", lambda e: e.drain())
            vc_v2 = p.go("vector", lambda e: e.tensor_sub(
                rowb[0:1, 256:512], rowb[0:1, 256:512], rowb[0:1, 512:768]),
                inc="vc")
            p.wait("scalar", "vc", vc_v2)
            sc_sd3 = p.go("scalar", lambda e: e.activation(
                rowb[0:1, 512:768], rowb[0:1, 256:512], AF.Sqrt,
                bias=cst[0:1, 1:2], scale=1.0), inc="sc")
            p.wait("vector", "sc", sc_sd3)
            p.go("vector", lambda e: e.reciprocal(
                rowb[0:1, 256:512], rowb[0:1, 512:768]), inc="vc")
            p.go("vector", lambda e: e.drain())
            vc_mi = p.go("vector", lambda e: e.tensor_mul(
                rowb[0:1, 512:768], rowb[0:1, 0:256], rowb[0:1, 256:512]),
                inc="vc")
            p.wait("tensor", "vc", vc_mi)
            tn_bc2 = p.go("tensor", lambda e: e.matmul(
                ps_sb[:, 0:512], onr[0:1, :], rowb[0:1, 256:768],
                start=True, stop=True), inc="tn")
            p.wait("vector", "tn", tn_bc2)
            vc_bc2 = p.go("vector", lambda e: e.tensor_copy(
                bc[:, 0:512], ps_sb[:, 0:512]), inc="vc")
            bfree["sb"] = [("vc", vc_bc2)]
            p.wait("vector", "tn", p.cnt["tn"])
            for s_ in range(KD):
                p.go("vector", lambda e, s_=s_: e.tensor_mul(
                    zt[:, s_ * TP:(s_ + 1) * TP], xt[:, s_ * TP:(s_ + 1) * TP],
                    bc[:, 0:256]), inc="vc")
            for s_ in range(KD):
                p.go("vector", lambda e, s_=s_: e.tensor_sub(
                    zt[:, s_ * TP:(s_ + 1) * TP], zt[:, s_ * TP:(s_ + 1) * TP],
                    bc[:, 256:512]), inc="vc")
            p.go("vector", lambda e: e.tensor_copy(lnh[:], zt[:]), inc="vc")
            vc_ln2 = p.go("vector", lambda e: e.tensor_sub(
                lnl[:], zt[:], lnh[:]), inc="vc")

            def ffn_stage(n_out, kk, hi_scr, lo_scr, rhs_h, rhs_l, dst_h,
                          dst_l, bias_sb, func, first_wait, h_free_gate,
                          resid=False):
                prev = None
                last_mm = None
                first = True
                sc_last = None
                vc_last = None
                t1_free = [0, 0]
                for s_ in range(n_out):
                    par = s_ % 2
                    fk = "mm0" if par == 0 else "mm1"
                    psap = ps_mm[par]
                    nhalf = kk // KD
                    if False:
                        nmm = 0
                        total = kk * 3
                        if nhalf == 1:
                            srcs = [hi_scr[s_, :, :], lo_scr[s_, :, :]]
                        else:
                            srcs = [hi_scr[s_, :, 0:D], hi_scr[s_, :, D:2 * D],
                                    lo_scr[s_, :, 0:D], lo_scr[s_, :, D:2 * D]]
                        slots, wdn = ringB.submit(srcs)
                        p.wait("tensor", "wd", wdn)
                        if first:
                            p.waits("tensor", first_wait)
                            first = False
                        p.waits("tensor", bfree[fk])

                        def lhs_tile(k):
                            if nhalf == 1:
                                return slots[0][:, k * 128:(k + 1) * 128], \
                                    slots[1][:, k * 128:(k + 1) * 128]
                            if k < 16:
                                return slots[0][:, k * 128:(k + 1) * 128], \
                                    slots[2][:, k * 128:(k + 1) * 128]
                            return slots[1][:, (k - 16) * 128:(k - 15) * 128], \
                                slots[3][:, (k - 16) * 128:(k - 15) * 128]
                        for combo in range(3):
                            for k in range(kk):
                                hi_t, lo_t = lhs_tile(k)
                                lhs = hi_t if combo < 2 else lo_t
                                rhs = rhs_h if combo in (0, 2) else rhs_l
                                last = (nmm == total - 1)
                                last_mm = p.go(
                                    "tensor", lambda e, psap=psap, lhs=lhs,
                                    rhs=rhs, k=k, st=(nmm == 0), sp=last:
                                    e.matmul(psap[:], lhs,
                                             rhs[:, k * TP:k * TP + TP],
                                             start=st, stop=sp),
                                    inc="tn" if last else None)
                                nmm += 1
                        ringB.consumed(len(srcs), [("tn", p.cnt["tn"])])
                    else:
                        nmm = 0
                        total = kk * 3
                        for half in range(nhalf):
                            srcs = [hi_scr[s_, :, half * D:(half + 1) * D],
                                    lo_scr[s_, :, half * D:(half + 1) * D]]
                            slots, wdn = ringB.submit(srcs)
                            p.wait("tensor", "wd", wdn)
                            if first:
                                p.waits("tensor", first_wait)
                                first = False
                            if half == 0:
                                p.waits("tensor", bfree[fk])
                            half_elems = KD * 3
                            for combo in range(3):
                                for kq in range(KD):
                                    k = half * KD + kq
                                    lhs_sl = slots[0] if combo < 2 else slots[1]
                                    lhs = lhs_sl[:, kq * 128:(kq + 1) * 128]
                                    rhs = rhs_h if combo in (0, 2) else rhs_l
                                    last = (nmm == total - 1)
                                    hend = (nmm % half_elems == half_elems - 1)
                                    last_mm = p.go(
                                        "tensor", lambda e, psap=psap, lhs=lhs,
                                        rhs=rhs, k=k, st=(nmm == 0), sp=last:
                                        e.matmul(psap[:], lhs,
                                                 rhs[:, k * TP:k * TP + TP],
                                                 start=st, stop=sp),
                                        inc="tn" if (last or hend) else None)
                                    nmm += 1
                            ringB.consumed(2, [("tn", p.cnt["tn"])])
                    p.wait("scalar", "tn", last_mm)
                    p.wait("scalar", "vc", t1_free[s_ % 2])
                    sc_last = p.go("scalar", lambda e, s_=s_, psap=psap:
                                   e.activation(
                                       t1[s_ % 2][:], psap[:], func,
                                       bias=bias_sb[:, s_:s_ + 1],
                                       scale=1.0), inc="sc")
                    bfree[fk] = [("sc", sc_last)]
                    p.wait("vector", "sc", sc_last)
                    if s_ == 0:
                        p.waits("vector", h_free_gate)
                    if resid:
                        vc_last = p.go("vector", lambda e, s_=s_:
                                       e.tensor_add(
                                           xt[:, s_ * TP:(s_ + 1) * TP],
                                           xt[:, s_ * TP:(s_ + 1) * TP],
                                           t1[s_ % 2][:]), inc="vc")
                        t1_free[s_ % 2] = vc_last
                    else:
                        vc_last = p.go("vector", lambda e, s_=s_: e.tensor_copy(
                            dst_h[:, s_ * TP:(s_ + 1) * TP], t1[s_ % 2][:]),
                            inc="vc")
                        if prev is not None:
                            ps_, buf_ = prev
                            vc_last = p.go(
                                "vector", lambda e, ps_=ps_, buf_=buf_:
                                e.tensor_sub(
                                    dst_l[:, ps_ * TP:(ps_ + 1) * TP],
                                    buf_[:],
                                    dst_h[:, ps_ * TP:(ps_ + 1) * TP]),
                                inc="vc")
                            t1_free[ps_ % 2] = vc_last
                        prev = (s_, t1[s_ % 2])
                if not resid and prev is not None:
                    ps_, buf_ = prev
                    p.go("vector", lambda e: e.drain())
                    vc_last = p.go("vector", lambda e, ps_=ps_, buf_=buf_:
                                   e.tensor_sub(
                                       dst_l[:, ps_ * TP:(ps_ + 1) * TP],
                                       buf_[:],
                                       dst_h[:, ps_ * TP:(ps_ + 1) * TP]),
                                   inc="vc")
                return last_mm, sc_last, vc_last

            mm1, _, vc_h1 = ffn_stage(
                KH, KD, w1h_d, w1l_d, lnh, lnl, h1h, h1l, b1s, AF.Relu,
                [("vc", vc_ln2)], gate["h1_free"])
            lnf_free_next = [("tn", p.cnt["tn"])]
            mm2, _, vc_h2 = ffn_stage(
                KH, KH, w2h_d, w2l_d, h1h, h1l, h2h, h2l, b2s, AF.Relu,
                [("vc", vc_h1)], gate["h2_free"])
            h1_free_next = [("tn", p.cnt["tn"])]
            mm3, sc3, vc_y2 = ffn_stage(
                KD, KH, w3h_d, w3l_d, h2h, h2l, None, None, b3s, AF.Identity,
                [("vc", vc_h2)], [], resid=True)
            h2_free_next = [("tn", p.cnt["tn"])]

            od_last = None
            for i in range(2):
                otiles = [xt[:, os_ * TP + i * 128:os_ * TP + (i + 1) * 128]
                          for os_ in range(KD)]
                dsto = lambda idx, i=i: (xna if i == 0 else xnb)[
                    :, idx * 128:(idx + 1) * 128]
                ev = transpose_pass(otiles, dsto,
                                    [("vc", vc_y2)], [("tn", tn_st2)])
                p.wait("gpsimd", "vc", ev)
                od_last = p.go("gpsimd", lambda e, i=i, r0=r0: e.dma_start(
                    out=y_d[r0 + i * 128:r0 + (i + 1) * 128, :],
                    in_=(xna if i == 0 else xnb)[:]), inc="od", n=16)

            gate["xn_free"] = [("od", od_last)]
            gate["xt_free"] = [("tn", p.cnt["tn"])]
            gate["lx_free"] = [("tn", tn_w["g"])]
            gate["lnf_free"] = lnf_free_next
            gate["hb_free"] = [("vc", p.cnt["vc"])]
            gate["gb_free"] = [("vc", vc_o)]
            gate["zt_free"] = [("vc", vc_ln2)]
            gate["h1_free"] = h1_free_next
            gate["h2_free"] = h2_free_next

        p.wait("gpsimd", "od", p.cnt["od"])
        p.wait("gpsimd", "wd", p.cnt["wd"])

        with nc.Block() as block:
            @block.tensor
            def _(eng):
                for fn in p.streams["tensor"]:
                    fn(eng)

            @block.vector
            def _(eng):
                for fn in p.streams["vector"]:
                    fn(eng)

            @block.scalar
            def _(eng):
                for fn in p.streams["scalar"]:
                    fn(eng)

            @block.gpsimd
            def _(eng):
                for fn in p.streams["gpsimd"]:
                    fn(eng)

            @block.sync
            def _(eng):
                for fn in p.streams["sync"]:
                    fn(eng)
    return nc


def _pack_vec(v, k):
    return np.ascontiguousarray(np.asarray(v, dtype=np.float32)
                                .reshape(k, 128).T)


def _tile_in(w, ks, km):
    w = np.asarray(w, dtype=np.float32)
    return np.ascontiguousarray(
        w.reshape(ks, 128, km, 128).transpose(0, 3, 2, 1)
        .reshape(ks, 128, km * 128))


def _tile_nat(w, ks, km):
    w = np.asarray(w, dtype=np.float32)
    return np.ascontiguousarray(
        w.reshape(km, 128, ks, 128).transpose(2, 1, 0, 3)
        .reshape(ks, 128, km * 128))


def kernel(**inp):
    x = np.asarray(inp["L0_attn_hidden_states"], dtype=np.float32)
    base = {
        "wit": _tile_nat(inp["L0_attn_w_i"], KD, KD),
        "wft": _tile_nat(inp["L0_attn_w_f"], KD, KD),
        "wgt": _tile_nat(inp["L0_attn_w_g"], KD, KD),
        "wot": _tile_in(inp["L0_attn_w_o"], KD, KD),
        "w1t": _tile_in(inp["L0_ffn_W1"], KH, KD),
        "w2t": _tile_in(inp["L0_ffn_W2"], KH, KH),
        "w3t": _tile_in(inp["L0_ffn_W3"], KD, KH),
        "svi": _pack_vec(inp["L0_attn_s_i"], KD),
        "svf": _pack_vec(inp["L0_attn_s_f"], KD),
        "svg": _pack_vec(inp["L0_attn_s_g"], KD),
        "svo": _pack_vec(inp["L0_attn_s_o"], KD),
        "gnw": _pack_vec(inp["L0_attn_g_norm_w"], KD),
        "b1m": _pack_vec(inp["L0_ffn_b1"], KH),
        "b2m": _pack_vec(inp["L0_ffn_b2"], KH),
        "b3m": _pack_vec(inp["L0_ffn_b3"], KD),
        "ident": np.eye(128, dtype=np.float32),
        "onec": np.ones((128, 1), dtype=np.float32),
        "oner": np.ones((1, 128), dtype=np.float32),
    }
    nc = build()
    in_maps = []
    for c in range(8):
        m = dict(base)
        m["x"] = np.ascontiguousarray(x[2 * c:2 * c + 2].reshape(NT, D))
        in_maps.append(m)
    res = run_bass_kernel_spmd(nc, in_maps, core_ids=list(range(8)))
    y = np.concatenate(
        [np.asarray(r["y"]).reshape(2, SEQ, D) for r in res.results], axis=0)
    return y


# version 07
# speedup vs baseline: 1.1289x; 1.1289x over previous
import os
import numpy as np
import concourse.bass as bass
import concourse.mybir as mybir
from concourse.bass_utils import run_bass_kernel_spmd
from contextlib import ExitStack

F32 = mybir.dt.float32
BF16 = mybir.dt.bfloat16
FP16 = mybir.dt.float16
AF = mybir.ActivationFunctionType
ALU = mybir.AluOpType
AX = mybir.AxisListType

D = 2048
H = 4096
TP = 256
SEQ = 2048
NT = 2 * SEQ
NCH = NT // TP
KD = D // 128
KH = H // 128
EPS = 1e-5
NSA = 6
NSB = 5

ENGS = ("tensor", "vector", "scalar", "gpsimd", "sync")


class Prog:
    def __init__(self, nc, es):
        self.nc = nc
        self.streams = {e: [] for e in ENGS}
        self.sems = {}
        self.cnt = {}
        for name in ("wd", "gd", "od", "tn", "vc", "sc"):
            self.sems[name] = es.enter_context(nc.semaphore(name))
            self.cnt[name] = 0

    def wait(self, eng, sem, val):
        if val <= 0:
            return
        s = self.sems[sem]
        self.streams[eng].append(lambda e, s=s, v=val: e.wait_ge(s, v))

    def waits(self, eng, lst):
        for sem, val in lst:
            self.wait(eng, sem, val)

    def go(self, eng, fn, inc=None, n=1):
        if inc is None:
            self.streams[eng].append(fn)
            return None
        self.cnt[inc] += n
        s = self.sems[inc]

        def run(e, fn=fn, s=s, n=n):
            fn(e).then_inc(s, n)
        self.streams[eng].append(run)
        return self.cnt[inc]


def build(nch=NCH):
    nc = bass.Bass("TRN2", target_bir_lowering=False, debug=False,
                   detect_race_conditions=False)
    x_d = nc.dram_tensor("x", [NT, D], F32, kind="ExternalInput").ap()
    y_d = nc.dram_tensor("y", [NT, D], F32, kind="ExternalOutput").ap()
    wit_d = nc.dram_tensor("wit", [KD, 128, D], F32, kind="ExternalInput").ap()
    wft_d = nc.dram_tensor("wft", [KD, 128, D], F32, kind="ExternalInput").ap()
    wgt_d = nc.dram_tensor("wgt", [KD, 128, D], F32, kind="ExternalInput").ap()
    wot_d = nc.dram_tensor("wot", [KD, 128, D], F32, kind="ExternalInput").ap()
    w1t_d = nc.dram_tensor("w1t", [KH, 128, D], F32, kind="ExternalInput").ap()
    w2t_d = nc.dram_tensor("w2t", [KH, 128, H], F32, kind="ExternalInput").ap()
    w3t_d = nc.dram_tensor("w3t", [KD, 128, H], F32, kind="ExternalInput").ap()
    svi_d = nc.dram_tensor("svi", [128, KD], F32, kind="ExternalInput").ap()
    svf_d = nc.dram_tensor("svf", [128, KD], F32, kind="ExternalInput").ap()
    svg_d = nc.dram_tensor("svg", [128, KD], F32, kind="ExternalInput").ap()
    svo_d = nc.dram_tensor("svo", [128, KD], F32, kind="ExternalInput").ap()
    gnw_d = nc.dram_tensor("gnw", [128, KD], F32, kind="ExternalInput").ap()
    b1_d = nc.dram_tensor("b1m", [128, KH], F32, kind="ExternalInput").ap()
    b2_d = nc.dram_tensor("b2m", [128, KH], F32, kind="ExternalInput").ap()
    b3_d = nc.dram_tensor("b3m", [128, KD], F32, kind="ExternalInput").ap()
    id_d = nc.dram_tensor("ident", [128, 128], F32, kind="ExternalInput").ap()
    onc_d = nc.dram_tensor("onec", [128, 1], F32, kind="ExternalInput").ap()
    onr_d = nc.dram_tensor("oner", [1, 128], F32, kind="ExternalInput").ap()
    wis_d = nc.dram_tensor("wis", [KD, 128, D], FP16).ap()
    wfs_d = nc.dram_tensor("wfs", [KD, 128, D], FP16).ap()
    wgs_d = nc.dram_tensor("wgs", [KD, 128, D], FP16).ap()
    wos_d = nc.dram_tensor("wos", [KD, 128, D], BF16).ap()
    w1h_d = nc.dram_tensor("w1h", [KH, 128, D], BF16).ap()
    w1l_d = nc.dram_tensor("w1l", [KH, 128, D], BF16).ap()
    w2h_d = nc.dram_tensor("w2h", [KH, 128, H], BF16).ap()
    w2l_d = nc.dram_tensor("w2l", [KH, 128, H], BF16).ap()
    w3h_d = nc.dram_tensor("w3h", [KD, 128, H], BF16).ap()
    w3l_d = nc.dram_tensor("w3l", [KD, 128, H], BF16).ap()

    es = ExitStack()
    with es:
        sb = lambda name, shape, dt=F32: es.enter_context(
            nc.sbuf_tensor(name, shape, dt))
        xna = sb("xna", [128, D])
        xnb = sb("xnb", [128, D])
        xt = sb("xt", [128, KD * TP])
        zt = sb("zt", [128, KD * TP])
        lx = sb("lx", [128, KD * TP], FP16)
        hb = sb("hb", [128, KD * TP])
        gb = sb("gb", [128, KD * TP])
        h1h = sb("h1h", [128, KH * TP], BF16)
        h1l = sb("h1l", [128, KH * TP], BF16)
        h2h = sb("h2h", [128, KH * TP], BF16)
        h2l = sb("h2l", [128, KH * TP], BF16)
        lnh = h2h
        lnl = h2l
        wpa = [sb(f"wpa{j}", [128, D], FP16) for j in range(NSA)]
        wpb = [sb(f"wpb{j}", [128, D], BF16) for j in range(NSB)]
        sv_i = sb("sv_i", [128, KD])
        sv_f = sb("sv_f", [128, KD])
        sv_g = sb("sv_g", [128, KD])
        sv_o = sb("sv_o", [128, KD])
        sv_gn = sb("sv_gn", [128, KD])
        b1s = sb("b1s", [128, KH])
        b2s = sb("b2s", [128, KH])
        b3s = sb("b3s", [128, KD])
        id32 = sb("id32", [128, 128])
        onc = sb("onc", [128, 1])
        onr = sb("onr", [1, 128])
        cst = sb("cst", [128, 2])
        svr = sb("svr", [128, 16])
        rowb = sb("rowb", [1, 768])
        bc = sb("bc", [128, 512])
        t1 = [sb(f"t1{j}", [128, TP]) for j in range(2)]
        gsg = t1
        carry = sb("carry", [128, KD])
        psb = lambda name: es.enter_context(
            nc.psum_tensor(name, [128, 512], F32))
        psh = lambda name: es.enter_context(
            nc.psum_tensor(name, [128, 256], F32))
        ps_i = psh("ps_i")
        ps_f = psh("ps_f")
        ps_g = psh("ps_g")
        ps_mm = [psh("ps_mm0"), psh("ps_mm1")]
        ps_trA = psb("ps_trA")
        ps_trB = psb("ps_trB")
        ps_sb = psb("ps_sb")
        tr_banks = [ps_trA, ps_trB]

        p = Prog(nc, es)

        for dst, src in ((sv_i, svi_d), (sv_f, svf_d), (sv_g, svg_d),
                         (sv_o, svo_d), (sv_gn, gnw_d), (b1s, b1_d),
                         (b2s, b2_d), (b3s, b3_d), (id32, id_d),
                         (onc, onc_d), (onr, onr_d)):
            p.go("sync", lambda e, d=dst, s=src: e.dma_start(out=d[:], in_=s),
                 inc="wd", n=16)
        wd_small = p.cnt["wd"]
        p.wait("vector", "wd", wd_small)
        p.go("vector", lambda e: e.memset(cst[:, 0:1], 4096.5), inc="vc")
        p.go("vector", lambda e: e.memset(cst[:, 1:2], EPS), inc="vc")
        p.go("vector", lambda e: e.tensor_scalar(
            sv_i[:], sv_i[:], 1.0 / 256.0, None, ALU.mult), inc="vc")
        p.go("vector", lambda e: e.tensor_scalar(
            sv_f[:], sv_f[:], 1.0 / 256.0, None, ALU.mult), inc="vc")
        p.go("vector", lambda e: e.tensor_scalar(
            sv_g[:], sv_g[:], 1.0 / 256.0, None, ALU.mult), inc="vc")
        p.go("vector", lambda e: e.tensor_mul(sv_gn[:], sv_gn[:], sv_o[:]),
             inc="vc")
        p.go("vector", lambda e: e.drain())

        pbuf = [xna, xnb]
        pbuf_rd = [0, 0]
        slot_rd = {}

        def prep_cast(src_t, n_slab, width, hi_scr, lo_scr, hi_slot, lo_slot):
            for s_ in range(n_slab):
                for half in range(width // D):
                    j = (s_ * (width // D) + half) % 2
                    p.wait("sync", "vc", pbuf_rd[j])
                    ld = p.go("sync", lambda e, j=j, s_=s_, half=half:
                              e.dma_start(
                                  out=pbuf[j][:],
                                  in_=src_t[s_, :, half * D:(half + 1) * D]),
                              inc="wd", n=16)
                    p.wait("vector", "wd", ld)
                    p.wait("vector", "wd", slot_rd.get(id(hi_slot), 0))
                    cv = p.go("vector", lambda e, j=j, sl=hi_slot:
                              e.tensor_copy(sl[:], pbuf[j][:]), inc="vc")
                    if lo_scr is not None:
                        p.wait("vector", "wd", slot_rd.get(id(lo_slot), 0))
                        cv = p.go("vector", lambda e, j=j, hsl=hi_slot,
                                  lsl=lo_slot:
                                  e.tensor_sub(lsl[:], pbuf[j][:], hsl[:]),
                                  inc="vc")
                    pbuf_rd[j] = cv
                    p.wait("sync", "vc", cv)
                    for sl, scr in ([(hi_slot, hi_scr)] +
                                    ([(lo_slot, lo_scr)] if lo_scr is not None
                                     else [])):
                        slot_rd[id(sl)] = p.go(
                            "sync", lambda e, sl=sl, scr=scr, s_=s_, half=half:
                            e.dma_start(
                                out=scr[s_, :, half * D:(half + 1) * D],
                                in_=sl[:]), inc="wd", n=16)

        prep_cast(wit_d, KD, D, wis_d, None, wpa[0], None)
        prep_cast(wft_d, KD, D, wfs_d, None, wpa[1], None)
        prep_cast(wgt_d, KD, D, wgs_d, None, wpa[2], None)
        prep_cast(wot_d, KD, D, wos_d, None, wpb[0], None)
        prep_cast(w1t_d, KH, D, w1h_d, w1l_d, wpb[1], wpb[2])
        prep_cast(w2t_d, KH, H, w2h_d, w2l_d, wpb[3], wpb[4])
        prep_cast(w3t_d, KD, H, w3h_d, w3l_d, wpb[3], wpb[1])

        prep_end = {s: p.cnt[s] for s in ("wd", "vc", "tn", "sc")}
        init_xn_free = [("vc", prep_end["vc"])]

        class Ring:
            def __init__(self, slots, init_waits):
                self.slots = slots
                self.n = len(slots)
                self.load_idx = 0
                self.freed = []
                self.init_waits = init_waits

            def submit(self, srcs):
                out = []
                for src in srcs:
                    slot = self.slots[self.load_idx % self.n]
                    if self.load_idx >= self.n:
                        p.waits("sync", self.freed[self.load_idx - self.n])
                    else:
                        p.waits("sync", self.init_waits)
                    p.go("sync", lambda e, d=slot, s=src:
                         e.dma_start(out=d[:], in_=s), inc="wd", n=16)
                    out.append(slot)
                    self.load_idx += 1
                return out, p.cnt["wd"]

            def consumed(self, n, waits):
                for _ in range(n):
                    self.freed.append(list(waits))

        ringA = Ring(wpa, [("wd", prep_end["wd"])])
        ringB = Ring(wpb, [("wd", prep_end["wd"])])

        bfree = {"i": [], "f": [], "g": [], "mm0": [], "mm1": [],
                 "trA": [], "trB": [], "sb": []}

        gate = {
            "xn_free": list(init_xn_free),
            "xt_free": [],
            "lx_free": [],
            "lnf_free": [],
            "hb_free": [],
            "gb_free": [],
            "zt_free": [],
            "h1_free": [],
            "h2_free": [],
        }

        gsg_free = [0, 0]

        def transpose_pass(src_tiles, dst_fn, waits_tensor, waits_vector):
            n = len(src_tiles)
            last_ev = None
            for r in range((n + 3) // 4):
                bank_i = r % 2
                bank = tr_banks[bank_i]
                key = "trA" if bank_i == 0 else "trB"
                p.waits("tensor", bfree[key])
                if r == 0:
                    p.waits("tensor", waits_tensor)
                tt = None
                for j in range(4):
                    idx = r * 4 + j
                    if idx >= n:
                        break
                    tt = p.go("tensor", lambda e, b=bank, j=j, s=src_tiles[idx]:
                              e.transpose(b[:, j * 128:(j + 1) * 128], s,
                                          id32[:]), inc="tn")
                p.wait("vector", "tn", tt)
                if r == 0:
                    p.waits("vector", waits_vector)
                ev = None
                for j in range(4):
                    idx = r * 4 + j
                    if idx >= n:
                        break
                    ev = p.go("vector", lambda e, b=bank, j=j, d=dst_fn(idx):
                              e.tensor_copy(d, b[:, j * 128:(j + 1) * 128]),
                              inc="vc")
                bfree[key] = [("vc", ev)]
                last_ev = ev
            return last_ev

        for c in range(nch):
            r0 = c * TP

            p.waits("gpsimd", gate["xn_free"])
            gd_x = None
            for i in range(2):
                gd_x = p.go("gpsimd", lambda e, i=i, r0=r0:
                            e.dma_start(out=(xna if i == 0 else xnb)[:],
                                        in_=x_d[r0 + i * 128:r0 + (i + 1) * 128, :]),
                            inc="gd", n=16)

            xtiles = [(xna if i == 0 else xnb)[:, kd * 128:(kd + 1) * 128]
                      for i in range(2) for kd in range(KD)]
            dstx = lambda idx: xt[:, (idx % KD) * TP + (idx // KD) * 128:
                                  (idx % KD) * TP + (idx // KD) * 128 + 128]
            p.wait("vector", "gd", gd_x)
            p.waits("vector", gate["xt_free"])
            transpose_pass(xtiles, dstx, [("gd", gd_x)] + gate["xt_free"], [])
            tn_xtr = p.cnt["tn"]
            for i in range(2):
                src = xna if i == 0 else xnb
                p.go("vector", lambda e, s=src, i=i:
                     e.reduce_sum(svr[:, i:i + 1], s[:], axis=AX.X), inc="vc")
            p.waits("scalar", [("gd", gd_x)] + gate["zt_free"])
            for i in range(2):
                src = xna if i == 0 else xnb
                sc_sq = p.go("scalar", lambda e, s=src, i=i:
                             e.activation(zt[:, i * D:(i + 1) * D], s[:],
                                          AF.Square), inc="sc")
            p.wait("vector", "sc", sc_sq)
            for i in range(2):
                p.go("vector", lambda e, i=i:
                     e.reduce_sum(svr[:, 2 + i:3 + i], zt[:, i * D:(i + 1) * D],
                                  axis=AX.X), inc="vc")
            p.go("vector", lambda e: e.drain())
            for i in range(2):
                p.go("vector", lambda e, i=i: e.tensor_scalar(
                    svr[:, 4 + i:5 + i], svr[:, i:i + 1], 1.0 / D, None,
                    ALU.mult), inc="vc")
            p.go("vector", lambda e: e.drain())
            for i in range(2):
                p.go("vector", lambda e, i=i: e.tensor_scalar(
                    svr[:, 6 + i:7 + i], svr[:, 2 + i:3 + i], 1.0 / D, None,
                    ALU.mult), inc="vc")
            p.go("vector", lambda e: e.drain())
            for i in range(2):
                p.go("vector", lambda e, i=i: e.tensor_mul(
                    svr[:, 8 + i:9 + i], svr[:, 4 + i:5 + i],
                    svr[:, 4 + i:5 + i]), inc="vc")
            p.go("vector", lambda e: e.drain())
            for i in range(2):
                vc_var = p.go("vector", lambda e, i=i: e.tensor_sub(
                    svr[:, 6 + i:7 + i], svr[:, 6 + i:7 + i],
                    svr[:, 8 + i:9 + i]), inc="vc")
            p.go("vector", lambda e: e.drain())
            p.wait("scalar", "vc", vc_var)
            for i in range(2):
                sc_sd = p.go("scalar", lambda e, i=i: e.activation(
                    svr[:, 8 + i:9 + i], svr[:, 6 + i:7 + i], AF.Sqrt,
                    bias=cst[:, 1:2], scale=1.0), inc="sc")
            p.wait("vector", "sc", sc_sd)
            for i in range(2):
                p.go("vector", lambda e, i=i: e.reciprocal(
                    svr[:, 10 + i:11 + i], svr[:, 8 + i:9 + i]), inc="vc")
            p.go("vector", lambda e: e.drain())
            for i in range(2):
                p.go("vector", lambda e, i=i: e.tensor_scalar(
                    svr[:, 12 + i:13 + i], svr[:, 10 + i:11 + i], 256.0, None,
                    ALU.mult), inc="vc")
            p.go("vector", lambda e: e.drain())
            for i in range(2):
                p.go("vector", lambda e, i=i: e.tensor_mul(
                    svr[:, 10 + i:11 + i], svr[:, 12 + i:13 + i],
                    svr[:, 4 + i:5 + i]), inc="vc")
            p.go("vector", lambda e: e.drain())
            for i in range(2):
                vc_rows = p.go("vector", lambda e, i=i: e.tensor_scalar(
                    svr[:, 14 + i:15 + i], svr[:, 10 + i:11 + i], -1.0, 2048.5,
                    ALU.mult, ALU.add), inc="vc")
            p.go("vector", lambda e: e.drain())
            p.wait("scalar", "vc", vc_rows)
            p.wait("scalar", "tn", tn_xtr)
            sc_ap = None
            for i in range(2):
                src = xna if i == 0 else xnb
                sc_ap = p.go("scalar", lambda e, s=src, i=i: e.activation(
                    s[:], s[:], AF.Identity, bias=svr[:, 14 + i:15 + i],
                    scale=svr[:, 12 + i:13 + i]), inc="sc")
            ztiles = [(xna if i == 0 else xnb)[:, kd * 128:(kd + 1) * 128]
                      for i in range(2) for kd in range(KD)]
            dstz = lambda idx: zt[:, (idx % KD) * TP + (idx // KD) * 128:
                                  (idx % KD) * TP + (idx // KD) * 128 + 128]
            transpose_pass(ztiles, dstz,
                           [("sc", sc_ap), ("vc", p.cnt["vc"])], [])
            p.waits("vector", [("tn", p.cnt["tn"])] + gate["lx_free"])
            for hh in range(2):
                zh = zt[:, hh * D:(hh + 1) * D]
                d_ = xna
                c_ = xnb
                p.go("vector", lambda e, zh=zh: e.tensor_copy(d_[:], zh),
                     inc="vc")
                for b in range(11, -1, -1):
                    p.go("vector", lambda e, b=b: e.tensor_scalar(
                        c_[:], d_[:], float(2 ** b), None, ALU.is_ge),
                        inc="vc")
                    p.go("vector", lambda e, b=b: e.scalar_tensor_tensor(
                        d_[:], c_[:], -float(2 ** b), d_[:], ALU.mult,
                        ALU.add), inc="vc")
                p.go("vector", lambda e, zh=zh: e.scalar_tensor_tensor(
                    zh, d_[:], -1.0, zh, ALU.mult, ALU.add), inc="vc")
                vc_lnf = p.go("vector", lambda e, zh=zh, hh=hh: e.tensor_scalar(
                    lx[:, hh * D:(hh + 1) * D], zh, -2048.0, None, ALU.add),
                    inc="vc")
            first_ifg = True
            sc_f_done = vc_ifg_done = None
            ATTNMONO = True
            for s_ in range(KD):
                tn_w = {}
                if ATTNMONO:
                    slots3, wdn3 = ringA.submit([wis_d[s_, :, :],
                                                 wfs_d[s_, :, :],
                                                 wgs_d[s_, :, :]])
                for wj, (scr, psap, key) in enumerate(
                        ((wis_d, ps_i, "i"),
                         (wfs_d, ps_f, "f"),
                         (wgs_d, ps_g, "g"))):
                    if ATTNMONO:
                        slot, wdn = slots3[wj], wdn3
                    else:
                        slots, wdn = ringA.submit([scr[s_, :, :]])
                        slot = slots[0]
                    p.wait("tensor", "wd", wdn)
                    if first_ifg:
                        p.wait("tensor", "vc", vc_lnf)
                        first_ifg = False
                    p.waits("tensor", bfree[key])
                    for kd in range(KD):
                        p.go("tensor", lambda e, psap=psap, slot=slot, kd=kd,
                             st=(kd == 0), sp=(kd == KD - 1):
                             e.matmul(psap[:],
                                      slot[:, kd * 128:(kd + 1) * 128],
                                      lx[:, kd * TP:kd * TP + TP],
                                      start=st, stop=sp),
                             inc="tn" if kd == KD - 1 else None)
                    tn_w[key] = p.cnt["tn"]
                    if not ATTNMONO:
                        ringA.consumed(1, [("tn", p.cnt["tn"])])
                if ATTNMONO:
                    ringA.consumed(3, [("tn", p.cnt["tn"])])
                p.wait("vector", "tn", tn_w["i"])
                if s_ == 0:
                    p.waits("vector", gate["hb_free"] + gate["gb_free"])
                e1 = p.go("vector", lambda e, s_=s_: e.tensor_scalar(
                    hb[:, s_ * TP:(s_ + 1) * TP], ps_i[:],
                    sv_i[:, s_:s_ + 1], None, ALU.mult), inc="vc")
                bfree["i"] = [("vc", e1)]
                p.wait("scalar", "tn", tn_w["f"])
                if s_ == 0:
                    p.waits("scalar", [("vc", vc_lnf)])
                e2 = p.go("scalar", lambda e, s_=s_: e.activation(
                    zt[:, s_ * TP:(s_ + 1) * TP], ps_f[:], AF.Sigmoid,
                    scale=sv_f[:, s_:s_ + 1]), inc="sc")
                bfree["f"] = [("sc", e2)]
                par = s_ % 2
                p.wait("vector", "tn", tn_w["g"])
                e4 = p.go("vector", lambda e, s_=s_: e.tensor_scalar(
                    gb[:, s_ * TP:(s_ + 1) * TP], ps_g[:],
                    sv_g[:, s_:s_ + 1], None, ALU.mult), inc="vc")
                bfree["g"] = [("vc", e4)]
                p.wait("scalar", "vc", e4)
                p.wait("scalar", "vc", gsg_free[par])
                e3 = p.go("scalar", lambda e, par=par, s_=s_: e.activation(
                    gsg[par][:], gb[:, s_ * TP:(s_ + 1) * TP], AF.Sigmoid),
                    inc="sc")
                p.wait("vector", "sc", e3)
                vc_ifg_done = p.go("vector", lambda e, s_=s_, par=par:
                                   e.tensor_mul(
                                       gb[:, s_ * TP:(s_ + 1) * TP],
                                       gb[:, s_ * TP:(s_ + 1) * TP],
                                       gsg[par][:]), inc="vc")
                gsg_free[par] = vc_ifg_done
                sc_f_done = e2
            p.wait("vector", "sc", sc_f_done)
            for hh in range(2):
                tmp = xna if hh == 0 else xnb
                p.go("vector", lambda e, hh=hh, tmp=tmp: e.tensor_mul(
                    tmp[:], zt[:, hh * D:(hh + 1) * D],
                    hb[:, hh * D:(hh + 1) * D]), inc="vc")
                p.go("vector", lambda e, hh=hh, tmp=tmp: e.tensor_sub(
                    hb[:, hh * D:(hh + 1) * D], hb[:, hh * D:(hh + 1) * D],
                    tmp[:]), inc="vc")
            for s_ in range(KD):
                init = (0.0 if c % (SEQ // TP) == 0 else carry[:, s_:s_ + 1])
                p.go("vector", lambda e, s_=s_, init=init:
                     e.tensor_tensor_scan(
                         hb[:, s_ * TP:(s_ + 1) * TP],
                         zt[:, s_ * TP:(s_ + 1) * TP],
                         hb[:, s_ * TP:(s_ + 1) * TP],
                         init, ALU.mult, ALU.add), inc="vc")
            p.go("vector", lambda e: e.drain())
            for s_ in range(KD):
                vc_carry = p.go("vector", lambda e, s_=s_: e.tensor_copy(
                    carry[:, s_:s_ + 1],
                    hb[:, s_ * TP + TP - 1:s_ * TP + TP]), inc="vc")
            p.wait("scalar", "vc", vc_carry)
            sq_sc = None
            for i in range(2):
                sq_sc = p.go("scalar", lambda e, i=i: e.activation(
                    (xna if i == 0 else xnb)[:], hb[:, i * D:(i + 1) * D],
                    AF.Square), inc="sc")
            p.wait("tensor", "sc", sq_sc)
            p.waits("tensor", bfree["sb"])
            for kd in range(KD):
                src = xna if kd < 8 else xnb
                tn_rms = p.go("tensor", lambda e, src=src, kd=kd: e.matmul(
                    ps_sb[0:1, 0:256], onc[:],
                    src[:, (kd % 8) * TP:(kd % 8 + 1) * TP],
                    start=(kd == 0), stop=(kd == KD - 1)),
                    inc="tn" if kd == KD - 1 else None)
            p.wait("scalar", "tn", tn_rms)
            sc_sd2 = p.go("scalar", lambda e: e.activation(
                rowb[0:1, 0:256], ps_sb[0:1, 0:256], AF.Sqrt,
                bias=cst[0:1, 1:2], scale=1.0 / D), inc="sc")
            p.wait("vector", "sc", sc_sd2)
            vc_rr = p.go("vector", lambda e: e.reciprocal(
                rowb[0:1, 256:512], rowb[0:1, 0:256]), inc="vc")
            p.wait("tensor", "vc", vc_rr)
            tn_bc = p.go("tensor", lambda e: e.matmul(
                ps_sb[:, 256:512], onr[0:1, :], rowb[0:1, 256:512],
                start=True, stop=True), inc="tn")
            p.wait("vector", "tn", tn_bc)
            vc_bc = p.go("vector", lambda e: e.tensor_copy(
                bc[:, 0:256], ps_sb[:, 256:512]), inc="vc")
            bfree["sb"] = [("vc", vc_bc)]
            p.wait("vector", "sc", sq_sc)
            for s_ in range(KD):
                p.go("vector", lambda e, s_=s_: e.tensor_mul(
                    hb[:, s_ * TP:(s_ + 1) * TP], hb[:, s_ * TP:(s_ + 1) * TP],
                    bc[:, 0:256]), inc="vc")
            p.go("vector", lambda e: e.tensor_mul(hb[:], hb[:], gb[:]),
                 inc="vc")
            for s_ in range(KD):
                p.go("vector", lambda e, s_=s_: e.tensor_scalar(
                    hb[:, s_ * TP:(s_ + 1) * TP], hb[:, s_ * TP:(s_ + 1) * TP],
                    sv_gn[:, s_:s_ + 1], None, ALU.mult), inc="vc")
            p.waits("vector", gate["h2_free"])
            p.go("vector", lambda e: e.tensor_copy(lnh[:, 0:KD * TP], hb[:]), inc="vc")
            vc_o = p.go("vector", lambda e: e.tensor_sub(lnl[:, 0:KD * TP], hb[:], lnh[:, 0:KD * TP]),
                        inc="vc")
            first_wo = True
            for os_ in range(KD):
                slots, wdn = ringB.submit([wos_d[os_, :, :]])
                par = os_ % 2
                wok = "mm0" if par == 0 else "mm1"
                psap = ps_mm[par]
                p.wait("tensor", "wd", wdn)
                if first_wo:
                    p.wait("tensor", "vc", vc_o)
                    first_wo = False
                p.waits("tensor", bfree[wok])
                nmm = 0
                for ke in range(KD):
                    for rhs in (lnh, lnl):
                        last = (ke == KD - 1 and rhs is lnl)
                        tn_wo = p.go("tensor", lambda e, psap=psap, ke=ke,
                                     rhs=rhs, st=(nmm == 0), sp=last,
                                     slot=slots[0]:
                                     e.matmul(psap[:],
                                              slot[:, ke * 128:(ke + 1) * 128],
                                              rhs[:, ke * TP:ke * TP + TP],
                                              start=st, stop=sp),
                                     inc="tn" if last else None)
                        nmm += 1
                ringB.consumed(1, [("tn", p.cnt["tn"])])
                p.wait("scalar", "tn", tn_wo)
                if os_ == 0:
                    p.wait("scalar", "vc", vc_carry)
                e4 = p.go("scalar", lambda e, os_=os_, psap=psap: e.activation(
                    zt[:, os_ * TP:(os_ + 1) * TP], psap[:], AF.Identity,
                    scale=256.0, bias=cst[:, 0:1]), inc="sc")
                bfree[wok] = [("sc", e4)]
            sc_wo = p.cnt["sc"]
            p.wait("vector", "sc", sc_wo)
            p.wait("vector", "tn", p.cnt["tn"])
            for hh in range(2):
                zh = zt[:, hh * D:(hh + 1) * D]
                d_ = xna
                c_ = xnb
                p.go("vector", lambda e, zh=zh: e.tensor_copy(d_[:], zh),
                     inc="vc")
                for b in range(12, -1, -1):
                    p.go("vector", lambda e, b=b: e.tensor_scalar(
                        c_[:], d_[:], float(2 ** b), None, ALU.is_ge), inc="vc")
                    p.go("vector", lambda e, b=b: e.scalar_tensor_tensor(
                        d_[:], c_[:], -float(2 ** b), d_[:], ALU.mult, ALU.add),
                        inc="vc")
                p.go("vector", lambda e, zh=zh: e.scalar_tensor_tensor(
                    zh, d_[:], -1.0, zh, ALU.mult, ALU.add), inc="vc")
                p.go("vector", lambda e, zh=zh: e.tensor_scalar(
                    zh, zh, 1.0 / 256.0, -16.0, ALU.mult, ALU.add), inc="vc")
                vc_y1 = p.go("vector", lambda e, zh=zh, hh=hh: e.tensor_add(
                    xt[:, hh * D:(hh + 1) * D], xt[:, hh * D:(hh + 1) * D],
                    zh), inc="vc")
            p.wait("scalar", "vc", vc_y1)
            sq2 = None
            for i in range(2):
                sq2 = p.go("scalar", lambda e, i=i: e.activation(
                    (xna if i == 0 else xnb)[:], xt[:, i * D:(i + 1) * D],
                    AF.Square), inc="sc")
            p.wait("tensor", "sc", sq2)
            p.waits("tensor", bfree["sb"])
            for kd in range(KD):
                p.go("tensor", lambda e, kd=kd: e.matmul(
                    ps_sb[0:1, 0:256], onc[:], xt[:, kd * TP:(kd + 1) * TP],
                    start=(kd == 0), stop=(kd == KD - 1)),
                    inc="tn" if kd == KD - 1 else None)
            for kd in range(KD):
                src = xna if kd < 8 else xnb
                tn_st2 = p.go("tensor", lambda e, src=src, kd=kd: e.matmul(
                    ps_sb[0:1, 256:512], onc[:],
                    src[:, (kd % 8) * TP:(kd % 8 + 1) * TP],
                    start=(kd == 0), stop=(kd == KD - 1)),
                    inc="tn" if kd == KD - 1 else None)
            p.wait("vector", "tn", tn_st2)
            p.go("vector", lambda e: e.tensor_scalar(
                rowb[0:1, 0:256], ps_sb[0:1, 0:256], 1.0 / D, None, ALU.mult),
                inc="vc")
            p.go("vector", lambda e: e.tensor_scalar(
                rowb[0:1, 256:512], ps_sb[0:1, 256:512], 1.0 / D, None,
                ALU.mult), inc="vc")
            p.go("vector", lambda e: e.drain())
            p.go("vector", lambda e: e.tensor_mul(
                rowb[0:1, 512:768], rowb[0:1, 0:256], rowb[0:1, 0:256]),
                inc="vc")
            p.go("vector", lambda e: e.drain())
            vc_v2 = p.go("vector", lambda e: e.tensor_sub(
                rowb[0:1, 256:512], rowb[0:1, 256:512], rowb[0:1, 512:768]),
                inc="vc")
            p.wait("scalar", "vc", vc_v2)
            sc_sd3 = p.go("scalar", lambda e: e.activation(
                rowb[0:1, 512:768], rowb[0:1, 256:512], AF.Sqrt,
                bias=cst[0:1, 1:2], scale=1.0), inc="sc")
            p.wait("vector", "sc", sc_sd3)
            p.go("vector", lambda e: e.reciprocal(
                rowb[0:1, 256:512], rowb[0:1, 512:768]), inc="vc")
            p.go("vector", lambda e: e.drain())
            vc_mi = p.go("vector", lambda e: e.tensor_mul(
                rowb[0:1, 512:768], rowb[0:1, 0:256], rowb[0:1, 256:512]),
                inc="vc")
            p.wait("tensor", "vc", vc_mi)
            tn_bc2 = p.go("tensor", lambda e: e.matmul(
                ps_sb[:, 0:512], onr[0:1, :], rowb[0:1, 256:768],
                start=True, stop=True), inc="tn")
            p.wait("vector", "tn", tn_bc2)
            vc_bc2 = p.go("vector", lambda e: e.tensor_copy(
                bc[:, 0:512], ps_sb[:, 0:512]), inc="vc")
            bfree["sb"] = [("vc", vc_bc2)]
            p.wait("vector", "tn", p.cnt["tn"])
            for s_ in range(KD):
                p.go("vector", lambda e, s_=s_: e.tensor_mul(
                    zt[:, s_ * TP:(s_ + 1) * TP], xt[:, s_ * TP:(s_ + 1) * TP],
                    bc[:, 0:256]), inc="vc")
            for s_ in range(KD):
                p.go("vector", lambda e, s_=s_: e.tensor_sub(
                    zt[:, s_ * TP:(s_ + 1) * TP], zt[:, s_ * TP:(s_ + 1) * TP],
                    bc[:, 256:512]), inc="vc")
            p.go("vector", lambda e: e.tensor_copy(lnh[:, 0:KD * TP], zt[:]), inc="vc")
            vc_ln2 = p.go("vector", lambda e: e.tensor_sub(
                lnl[:, 0:KD * TP], zt[:], lnh[:, 0:KD * TP]), inc="vc")

            def ffn_stage(n_out, kk, hi_scr, lo_scr, rhs_h, rhs_l, dst_h,
                          dst_l, bias_sb, func, first_wait, h_free_gate,
                          resid=False):
                prev = None
                last_mm = None
                first = True
                sc_last = None
                vc_last = None
                t1_free = [0, 0]
                for s_ in range(n_out):
                    par = s_ % 2
                    fk = "mm0" if par == 0 else "mm1"
                    psap = ps_mm[par]
                    nhalf = kk // KD
                    if False:
                        nmm = 0
                        total = kk * 3
                        if nhalf == 1:
                            srcs = [hi_scr[s_, :, :], lo_scr[s_, :, :]]
                        else:
                            srcs = [hi_scr[s_, :, 0:D], hi_scr[s_, :, D:2 * D],
                                    lo_scr[s_, :, 0:D], lo_scr[s_, :, D:2 * D]]
                        slots, wdn = ringB.submit(srcs)
                        p.wait("tensor", "wd", wdn)
                        if first:
                            p.waits("tensor", first_wait)
                            first = False
                        p.waits("tensor", bfree[fk])

                        def lhs_tile(k):
                            if nhalf == 1:
                                return slots[0][:, k * 128:(k + 1) * 128], \
                                    slots[1][:, k * 128:(k + 1) * 128]
                            if k < 16:
                                return slots[0][:, k * 128:(k + 1) * 128], \
                                    slots[2][:, k * 128:(k + 1) * 128]
                            return slots[1][:, (k - 16) * 128:(k - 15) * 128], \
                                slots[3][:, (k - 16) * 128:(k - 15) * 128]
                        for combo in range(3):
                            for k in range(kk):
                                hi_t, lo_t = lhs_tile(k)
                                lhs = hi_t if combo < 2 else lo_t
                                rhs = rhs_h if combo in (0, 2) else rhs_l
                                last = (nmm == total - 1)
                                last_mm = p.go(
                                    "tensor", lambda e, psap=psap, lhs=lhs,
                                    rhs=rhs, k=k, st=(nmm == 0), sp=last:
                                    e.matmul(psap[:], lhs,
                                             rhs[:, k * TP:k * TP + TP],
                                             start=st, stop=sp),
                                    inc="tn" if last else None)
                                nmm += 1
                        ringB.consumed(len(srcs), [("tn", p.cnt["tn"])])
                    else:
                        nmm = 0
                        total = kk * 3
                        for half in range(nhalf):
                            srcs = [hi_scr[s_, :, half * D:(half + 1) * D],
                                    lo_scr[s_, :, half * D:(half + 1) * D]]
                            slots, wdn = ringB.submit(srcs)
                            p.wait("tensor", "wd", wdn)
                            if first:
                                p.waits("tensor", first_wait)
                                first = False
                            if half == 0:
                                p.waits("tensor", bfree[fk])
                            half_elems = KD * 3
                            for combo in range(3):
                                for kq in range(KD):
                                    k = half * KD + kq
                                    lhs_sl = slots[0] if combo < 2 else slots[1]
                                    lhs = lhs_sl[:, kq * 128:(kq + 1) * 128]
                                    rhs = rhs_h if combo in (0, 2) else rhs_l
                                    last = (nmm == total - 1)
                                    hend = (nmm % half_elems == half_elems - 1)
                                    last_mm = p.go(
                                        "tensor", lambda e, psap=psap, lhs=lhs,
                                        rhs=rhs, k=k, st=(nmm == 0), sp=last:
                                        e.matmul(psap[:], lhs,
                                                 rhs[:, k * TP:k * TP + TP],
                                                 start=st, stop=sp),
                                        inc="tn" if (last or hend) else None)
                                    nmm += 1
                            ringB.consumed(2, [("tn", p.cnt["tn"])])
                    p.wait("scalar", "tn", last_mm)
                    p.wait("scalar", "vc", t1_free[s_ % 2])
                    sc_last = p.go("scalar", lambda e, s_=s_, psap=psap:
                                   e.activation(
                                       t1[s_ % 2][:], psap[:], func,
                                       bias=bias_sb[:, s_:s_ + 1],
                                       scale=1.0), inc="sc")
                    bfree[fk] = [("sc", sc_last)]
                    p.wait("vector", "sc", sc_last)
                    if s_ == 0:
                        p.waits("vector", h_free_gate)
                    if resid:
                        vc_last = p.go("vector", lambda e, s_=s_:
                                       e.tensor_add(
                                           xt[:, s_ * TP:(s_ + 1) * TP],
                                           xt[:, s_ * TP:(s_ + 1) * TP],
                                           t1[s_ % 2][:]), inc="vc")
                        t1_free[s_ % 2] = vc_last
                    else:
                        vc_last = p.go("vector", lambda e, s_=s_: e.tensor_copy(
                            dst_h[:, s_ * TP:(s_ + 1) * TP], t1[s_ % 2][:]),
                            inc="vc")
                        if prev is not None:
                            ps_, buf_ = prev
                            vc_last = p.go(
                                "vector", lambda e, ps_=ps_, buf_=buf_:
                                e.tensor_sub(
                                    dst_l[:, ps_ * TP:(ps_ + 1) * TP],
                                    buf_[:],
                                    dst_h[:, ps_ * TP:(ps_ + 1) * TP]),
                                inc="vc")
                            t1_free[ps_ % 2] = vc_last
                        prev = (s_, t1[s_ % 2])
                if not resid and prev is not None:
                    ps_, buf_ = prev
                    p.go("vector", lambda e: e.drain())
                    vc_last = p.go("vector", lambda e, ps_=ps_, buf_=buf_:
                                   e.tensor_sub(
                                       dst_l[:, ps_ * TP:(ps_ + 1) * TP],
                                       buf_[:],
                                       dst_h[:, ps_ * TP:(ps_ + 1) * TP]),
                                   inc="vc")
                return last_mm, sc_last, vc_last

            mm1, _, vc_h1 = ffn_stage(
                KH, KD, w1h_d, w1l_d, lnh, lnl, h1h, h1l, b1s, AF.Relu,
                [("vc", vc_ln2)], gate["h1_free"])
            lnf_free_next = [("tn", p.cnt["tn"])]
            mm2, _, vc_h2 = ffn_stage(
                KH, KH, w2h_d, w2l_d, h1h, h1l, h2h, h2l, b2s, AF.Relu,
                [("vc", vc_h1)], gate["h2_free"])
            h1_free_next = [("tn", p.cnt["tn"])]
            mm3, sc3, vc_y2 = ffn_stage(
                KD, KH, w3h_d, w3l_d, h2h, h2l, None, None, b3s, AF.Identity,
                [("vc", vc_h2)], [], resid=True)
            h2_free_next = [("tn", p.cnt["tn"])]

            od_last = None
            for i in range(2):
                otiles = [xt[:, os_ * TP + i * 128:os_ * TP + (i + 1) * 128]
                          for os_ in range(KD)]
                dsto = lambda idx, i=i: (xna if i == 0 else xnb)[
                    :, idx * 128:(idx + 1) * 128]
                ev = transpose_pass(otiles, dsto,
                                    [("vc", vc_y2)], [("tn", tn_st2)])
                p.wait("gpsimd", "vc", ev)
                od_last = p.go("gpsimd", lambda e, i=i, r0=r0: e.dma_start(
                    out=y_d[r0 + i * 128:r0 + (i + 1) * 128, :],
                    in_=(xna if i == 0 else xnb)[:]), inc="od", n=16)

            gate["xn_free"] = [("od", od_last)]
            gate["xt_free"] = [("tn", p.cnt["tn"])]
            gate["lx_free"] = [("tn", tn_w["g"])]
            gate["lnf_free"] = lnf_free_next
            gate["hb_free"] = [("vc", p.cnt["vc"])]
            gate["gb_free"] = [("vc", vc_o)]
            gate["zt_free"] = [("vc", vc_ln2)]
            gate["h1_free"] = h1_free_next
            gate["h2_free"] = h2_free_next

        p.wait("gpsimd", "od", p.cnt["od"])
        p.wait("gpsimd", "wd", p.cnt["wd"])

        with nc.Block() as block:
            @block.tensor
            def _(eng):
                for fn in p.streams["tensor"]:
                    fn(eng)

            @block.vector
            def _(eng):
                for fn in p.streams["vector"]:
                    fn(eng)

            @block.scalar
            def _(eng):
                for fn in p.streams["scalar"]:
                    fn(eng)

            @block.gpsimd
            def _(eng):
                for fn in p.streams["gpsimd"]:
                    fn(eng)

            @block.sync
            def _(eng):
                for fn in p.streams["sync"]:
                    fn(eng)
    return nc


def _pack_vec(v, k):
    return np.ascontiguousarray(np.asarray(v, dtype=np.float32)
                                .reshape(k, 128).T)


def _tile_in(w, ks, km):
    w = np.asarray(w, dtype=np.float32)
    return np.ascontiguousarray(
        w.reshape(ks, 128, km, 128).transpose(0, 3, 2, 1)
        .reshape(ks, 128, km * 128))


def _tile_nat(w, ks, km):
    w = np.asarray(w, dtype=np.float32)
    return np.ascontiguousarray(
        w.reshape(km, 128, ks, 128).transpose(2, 1, 0, 3)
        .reshape(ks, 128, km * 128))


def kernel(**inp):
    x = np.asarray(inp["L0_attn_hidden_states"], dtype=np.float32)
    base = {
        "wit": _tile_nat(inp["L0_attn_w_i"], KD, KD),
        "wft": _tile_nat(inp["L0_attn_w_f"], KD, KD),
        "wgt": _tile_nat(inp["L0_attn_w_g"], KD, KD),
        "wot": _tile_in(inp["L0_attn_w_o"], KD, KD),
        "w1t": _tile_in(inp["L0_ffn_W1"], KH, KD),
        "w2t": _tile_in(inp["L0_ffn_W2"], KH, KH),
        "w3t": _tile_in(inp["L0_ffn_W3"], KD, KH),
        "svi": _pack_vec(inp["L0_attn_s_i"], KD),
        "svf": _pack_vec(inp["L0_attn_s_f"], KD),
        "svg": _pack_vec(inp["L0_attn_s_g"], KD),
        "svo": _pack_vec(inp["L0_attn_s_o"], KD),
        "gnw": _pack_vec(inp["L0_attn_g_norm_w"], KD),
        "b1m": _pack_vec(inp["L0_ffn_b1"], KH),
        "b2m": _pack_vec(inp["L0_ffn_b2"], KH),
        "b3m": _pack_vec(inp["L0_ffn_b3"], KD),
        "ident": np.eye(128, dtype=np.float32),
        "onec": np.ones((128, 1), dtype=np.float32),
        "oner": np.ones((1, 128), dtype=np.float32),
    }
    nc = build()
    in_maps = []
    for c in range(8):
        m = dict(base)
        m["x"] = np.ascontiguousarray(x[2 * c:2 * c + 2].reshape(NT, D))
        in_maps.append(m)
    res = run_bass_kernel_spmd(nc, in_maps, core_ids=list(range(8)))
    y = np.concatenate(
        [np.asarray(r["y"]).reshape(2, SEQ, D) for r in res.results], axis=0)
    return y


# version 08
# speedup vs baseline: 1.2598x; 1.1160x over previous
import os
import numpy as np
import concourse.bass as bass
import concourse.mybir as mybir
from concourse.bass_utils import run_bass_kernel_spmd
from contextlib import ExitStack

F32 = mybir.dt.float32
BF16 = mybir.dt.bfloat16
FP16 = mybir.dt.float16
AF = mybir.ActivationFunctionType
ALU = mybir.AluOpType
AX = mybir.AxisListType

D = 2048
H = 4096
TP = 256
SEQ = 2048
NT = 2 * SEQ
NCH = NT // TP
KD = D // 128
KH = H // 128
EPS = 1e-5
NSA = 6
NSB = 5

ENGS = ("tensor", "vector", "scalar", "gpsimd", "sync")


class Prog:
    def __init__(self, nc, es):
        self.nc = nc
        self.streams = {e: [] for e in ENGS}
        self.sems = {}
        self.cnt = {}
        for name in ("wd", "gd", "od", "tn", "vc", "sc"):
            self.sems[name] = es.enter_context(nc.semaphore(name))
            self.cnt[name] = 0

    def wait(self, eng, sem, val):
        if val <= 0:
            return
        s = self.sems[sem]
        self.streams[eng].append(lambda e, s=s, v=val: e.wait_ge(s, v))

    def waits(self, eng, lst):
        for sem, val in lst:
            self.wait(eng, sem, val)

    def go(self, eng, fn, inc=None, n=1):
        if inc is None:
            self.streams[eng].append(fn)
            return None
        self.cnt[inc] += n
        s = self.sems[inc]

        def run(e, fn=fn, s=s, n=n):
            fn(e).then_inc(s, n)
        self.streams[eng].append(run)
        return self.cnt[inc]


def build(nch=NCH):
    nc = bass.Bass("TRN2", target_bir_lowering=False, debug=False,
                   detect_race_conditions=False)
    x_d = nc.dram_tensor("x", [NT, D], F32, kind="ExternalInput").ap()
    y_d = nc.dram_tensor("y", [NT, D], F32, kind="ExternalOutput").ap()
    wit_d = nc.dram_tensor("wit", [KD, 128, D], F32, kind="ExternalInput").ap()
    wft_d = nc.dram_tensor("wft", [KD, 128, D], F32, kind="ExternalInput").ap()
    wgt_d = nc.dram_tensor("wgt", [KD, 128, D], F32, kind="ExternalInput").ap()
    wot_d = nc.dram_tensor("wot", [KD, 128, D], F32, kind="ExternalInput").ap()
    w1t_d = nc.dram_tensor("w1t", [KH, 128, D], F32, kind="ExternalInput").ap()
    w2t_d = nc.dram_tensor("w2t", [KH, 128, H], F32, kind="ExternalInput").ap()
    w3t_d = nc.dram_tensor("w3t", [KD, 128, H], F32, kind="ExternalInput").ap()
    svi_d = nc.dram_tensor("svi", [128, KD], F32, kind="ExternalInput").ap()
    svf_d = nc.dram_tensor("svf", [128, KD], F32, kind="ExternalInput").ap()
    svg_d = nc.dram_tensor("svg", [128, KD], F32, kind="ExternalInput").ap()
    svo_d = nc.dram_tensor("svo", [128, KD], F32, kind="ExternalInput").ap()
    gnw_d = nc.dram_tensor("gnw", [128, KD], F32, kind="ExternalInput").ap()
    b1_d = nc.dram_tensor("b1m", [128, KH], F32, kind="ExternalInput").ap()
    b2_d = nc.dram_tensor("b2m", [128, KH], F32, kind="ExternalInput").ap()
    b3_d = nc.dram_tensor("b3m", [128, KD], F32, kind="ExternalInput").ap()
    id_d = nc.dram_tensor("ident", [128, 128], F32, kind="ExternalInput").ap()
    onc_d = nc.dram_tensor("onec", [128, 1], F32, kind="ExternalInput").ap()
    onr_d = nc.dram_tensor("oner", [1, 128], F32, kind="ExternalInput").ap()
    wis_d = nc.dram_tensor("wis", [KD, 128, D], FP16).ap()
    wfs_d = nc.dram_tensor("wfs", [KD, 128, D], FP16).ap()
    wgs_d = nc.dram_tensor("wgs", [KD, 128, D], FP16).ap()
    wos_d = nc.dram_tensor("wos", [KD, 128, D], BF16).ap()
    w1h_d = nc.dram_tensor("w1h", [KH, 128, D], BF16).ap()
    w1l_d = nc.dram_tensor("w1l", [KH, 128, D], BF16).ap()
    w2h_d = nc.dram_tensor("w2h", [KH, 128, H], BF16).ap()
    w2l_d = nc.dram_tensor("w2l", [KH, 128, H], BF16).ap()
    w3h_d = nc.dram_tensor("w3h", [KD, 128, H], BF16).ap()
    w3l_d = nc.dram_tensor("w3l", [KD, 128, H], BF16).ap()

    es = ExitStack()
    with es:
        sb = lambda name, shape, dt=F32: es.enter_context(
            nc.sbuf_tensor(name, shape, dt))
        xna = sb("xna", [128, D])
        xnb = sb("xnb", [128, D])
        xt = sb("xt", [128, KD * TP])
        zt = sb("zt", [128, KD * TP])
        lx = sb("lx", [128, KD * TP], FP16)
        hb = sb("hb", [128, KD * TP])
        gb = sb("gb", [128, KD * TP])
        h1h = sb("h1h", [128, KH * TP], BF16)
        h1l = sb("h1l", [128, KH * TP], BF16)
        h2h = sb("h2h", [128, KH * TP], BF16)
        h2l = sb("h2l", [128, KH * TP], BF16)
        lnh = h2h
        lnl = h2l
        wpa = [sb(f"wpa{j}", [128, D], FP16) for j in range(NSA)]
        wpb = [sb(f"wpb{j}", [128, D], BF16) for j in range(NSB)]
        sv_i = sb("sv_i", [128, KD])
        sv_f = sb("sv_f", [128, KD])
        sv_g = sb("sv_g", [128, KD])
        sv_o = sb("sv_o", [128, KD])
        sv_gn = sb("sv_gn", [128, KD])
        b1s = sb("b1s", [128, KH])
        b2s = sb("b2s", [128, KH])
        b3s = sb("b3s", [128, KD])
        id32 = sb("id32", [128, 128])
        onc = sb("onc", [128, 1])
        onr = sb("onr", [1, 128])
        cst = sb("cst", [128, 2])
        svr = sb("svr", [128, 16])
        rowb = sb("rowb", [1, 768])
        bc = sb("bc", [128, 512])
        t1 = [sb(f"t1{j}", [128, TP]) for j in range(2)]
        gsg = t1
        carry = sb("carry", [128, KD])
        psb = lambda name: es.enter_context(
            nc.psum_tensor(name, [128, 512], F32))
        psh = lambda name: es.enter_context(
            nc.psum_tensor(name, [128, 256], F32))
        ps_i = psh("ps_i")
        ps_f = psh("ps_f")
        ps_g = psh("ps_g")
        ps_mm = [psh("ps_mm0"), psh("ps_mm1")]
        ps_trA = psb("ps_trA")
        ps_trB = psb("ps_trB")
        ps_sb = psb("ps_sb")
        tr_banks = [ps_trA, ps_trB]

        p = Prog(nc, es)

        for dst, src in ((sv_i, svi_d), (sv_f, svf_d), (sv_g, svg_d),
                         (sv_o, svo_d), (sv_gn, gnw_d), (b1s, b1_d),
                         (b2s, b2_d), (b3s, b3_d), (id32, id_d),
                         (onc, onc_d), (onr, onr_d)):
            p.go("sync", lambda e, d=dst, s=src: e.dma_start(out=d[:], in_=s),
                 inc="wd", n=16)
        wd_small = p.cnt["wd"]
        p.wait("vector", "wd", wd_small)
        p.go("vector", lambda e: e.memset(cst[:, 0:1], 4096.5), inc="vc")
        p.go("vector", lambda e: e.memset(cst[:, 1:2], EPS), inc="vc")
        p.go("vector", lambda e: e.tensor_scalar(
            sv_i[:], sv_i[:], 1.0 / 256.0, None, ALU.mult), inc="vc")
        p.go("vector", lambda e: e.tensor_scalar(
            sv_f[:], sv_f[:], 1.0 / 256.0, None, ALU.mult), inc="vc")
        p.go("vector", lambda e: e.tensor_scalar(
            sv_g[:], sv_g[:], 1.0 / 256.0, None, ALU.mult), inc="vc")
        p.go("vector", lambda e: e.tensor_mul(sv_gn[:], sv_gn[:], sv_o[:]),
             inc="vc")
        p.go("vector", lambda e: e.drain())

        pbuf = [xna, xnb]
        pbuf_rd = [0, 0]
        slot_rd = {}

        def prep_cast(src_t, n_slab, width, hi_scr, lo_scr, hi_slot, lo_slot):
            for s_ in range(n_slab):
                for half in range(width // D):
                    j = (s_ * (width // D) + half) % 2
                    p.wait("sync", "vc", pbuf_rd[j])
                    ld = p.go("sync", lambda e, j=j, s_=s_, half=half:
                              e.dma_start(
                                  out=pbuf[j][:],
                                  in_=src_t[s_, :, half * D:(half + 1) * D]),
                              inc="wd", n=16)
                    p.wait("vector", "wd", ld)
                    p.wait("vector", "wd", slot_rd.get(id(hi_slot), 0))
                    cv = p.go("vector", lambda e, j=j, sl=hi_slot:
                              e.tensor_copy(sl[:], pbuf[j][:]), inc="vc")
                    if lo_scr is not None:
                        p.wait("vector", "wd", slot_rd.get(id(lo_slot), 0))
                        cv = p.go("vector", lambda e, j=j, hsl=hi_slot,
                                  lsl=lo_slot:
                                  e.tensor_sub(lsl[:], pbuf[j][:], hsl[:]),
                                  inc="vc")
                    pbuf_rd[j] = cv
                    p.wait("sync", "vc", cv)
                    for sl, scr in ([(hi_slot, hi_scr)] +
                                    ([(lo_slot, lo_scr)] if lo_scr is not None
                                     else [])):
                        slot_rd[id(sl)] = p.go(
                            "sync", lambda e, sl=sl, scr=scr, s_=s_, half=half:
                            e.dma_start(
                                out=scr[s_, :, half * D:(half + 1) * D],
                                in_=sl[:]), inc="wd", n=16)

        prep_cast(wit_d, KD, D, wis_d, None, wpa[0], None)
        prep_cast(wft_d, KD, D, wfs_d, None, wpa[1], None)
        prep_cast(wgt_d, KD, D, wgs_d, None, wpa[2], None)
        prep_cast(wot_d, KD, D, wos_d, None, wpb[0], None)
        prep_cast(w1t_d, KH, D, w1h_d, w1l_d, wpb[1], wpb[2])
        prep_cast(w2t_d, KH, H, w2h_d, w2l_d, wpb[3], wpb[4])
        prep_cast(w3t_d, KD, H, w3h_d, w3l_d, wpb[3], wpb[1])

        prep_end = {s: p.cnt[s] for s in ("wd", "vc", "tn", "sc")}
        init_xn_free = [("vc", prep_end["vc"])]

        class Ring:
            def __init__(self, slots, init_waits):
                self.slots = slots
                self.n = len(slots)
                self.load_idx = 0
                self.freed = []
                self.init_waits = init_waits

            def submit(self, srcs):
                out = []
                for src in srcs:
                    slot = self.slots[self.load_idx % self.n]
                    if self.load_idx >= self.n:
                        p.waits("sync", self.freed[self.load_idx - self.n])
                    else:
                        p.waits("sync", self.init_waits)
                    p.go("sync", lambda e, d=slot, s=src:
                         e.dma_start(out=d[:], in_=s), inc="wd", n=16)
                    out.append(slot)
                    self.load_idx += 1
                return out, p.cnt["wd"]

            def consumed(self, n, waits):
                for _ in range(n):
                    self.freed.append(list(waits))

        ringA = Ring(wpa, [("wd", prep_end["wd"])])
        ringB = Ring(wpb, [("wd", prep_end["wd"])])

        bfree = {"i": [], "f": [], "g": [], "mm0": [], "mm1": [],
                 "trA": [], "trB": [], "sb": []}

        gate = {
            "xn_free": list(init_xn_free),
            "xt_free": [],
            "lx_free": [],
            "lnf_free": [],
            "hb_free": [],
            "gb_free": [],
            "zt_free": [],
            "h1_free": [],
            "h2_free": [],
        }

        gsg_free = [0, 0]

        def transpose_pass(src_tiles, dst_fn, waits_tensor, waits_vector):
            n = len(src_tiles)
            last_ev = None
            for r in range((n + 3) // 4):
                bank_i = r % 2
                bank = tr_banks[bank_i]
                key = "trA" if bank_i == 0 else "trB"
                p.waits("tensor", bfree[key])
                if r == 0:
                    p.waits("tensor", waits_tensor)
                tt = None
                for j in range(4):
                    idx = r * 4 + j
                    if idx >= n:
                        break
                    tt = p.go("tensor", lambda e, b=bank, j=j, s=src_tiles[idx]:
                              e.transpose(b[:, j * 128:(j + 1) * 128], s,
                                          id32[:]), inc="tn")
                p.wait("vector", "tn", tt)
                if r == 0:
                    p.waits("vector", waits_vector)
                ev = None
                for j in range(4):
                    idx = r * 4 + j
                    if idx >= n:
                        break
                    ev = p.go("vector", lambda e, b=bank, j=j, d=dst_fn(idx):
                              e.tensor_copy(d, b[:, j * 128:(j + 1) * 128]),
                              inc="vc")
                bfree[key] = [("vc", ev)]
                last_ev = ev
            return last_ev

        def emit_prologue(c, vc_gbhb_free, vc_zt_free, tn_xnat_free,
                          apply_extra_waits):
            r0 = c * TP
            p.wait("gpsimd", "vc", vc_gbhb_free)
            gd_x = None
            for i in range(2):
                gd_x = p.go("gpsimd", lambda e, i=i, r0=r0:
                            e.dma_start(out=gb[:, i * D:(i + 1) * D],
                                        in_=x_d[r0 + i * 128:r0 + (i + 1) * 128, :]),
                            inc="gd", n=16)
            p.wait("vector", "gd", gd_x)
            for i in range(2):
                p.go("vector", lambda e, i=i:
                     e.reduce_sum(svr[:, i:i + 1], gb[:, i * D:(i + 1) * D],
                                  axis=AX.X), inc="vc")
            p.waits("scalar", [("gd", gd_x), ("vc", vc_zt_free)])
            sc_sq = None
            for i in range(2):
                sc_sq = p.go("scalar", lambda e, i=i:
                             e.activation(zt[:, i * D:(i + 1) * D],
                                          gb[:, i * D:(i + 1) * D],
                                          AF.Square), inc="sc")
            p.wait("vector", "sc", sc_sq)
            for i in range(2):
                p.go("vector", lambda e, i=i:
                     e.reduce_sum(svr[:, 2 + i:3 + i], zt[:, i * D:(i + 1) * D],
                                  axis=AX.X), inc="vc")
            p.go("vector", lambda e: e.drain())
            for i in range(2):
                p.go("vector", lambda e, i=i: e.tensor_scalar(
                    svr[:, 4 + i:5 + i], svr[:, i:i + 1], 1.0 / D, None,
                    ALU.mult), inc="vc")
            p.go("vector", lambda e: e.drain())
            for i in range(2):
                p.go("vector", lambda e, i=i: e.tensor_scalar(
                    svr[:, 6 + i:7 + i], svr[:, 2 + i:3 + i], 1.0 / D, None,
                    ALU.mult), inc="vc")
            p.go("vector", lambda e: e.drain())
            for i in range(2):
                p.go("vector", lambda e, i=i: e.tensor_mul(
                    svr[:, 8 + i:9 + i], svr[:, 4 + i:5 + i],
                    svr[:, 4 + i:5 + i]), inc="vc")
            p.go("vector", lambda e: e.drain())
            for i in range(2):
                vc_var = p.go("vector", lambda e, i=i: e.tensor_sub(
                    svr[:, 6 + i:7 + i], svr[:, 6 + i:7 + i],
                    svr[:, 8 + i:9 + i]), inc="vc")
            p.go("vector", lambda e: e.drain())
            p.wait("scalar", "vc", vc_var)
            for i in range(2):
                sc_sd = p.go("scalar", lambda e, i=i: e.activation(
                    svr[:, 8 + i:9 + i], svr[:, 6 + i:7 + i], AF.Sqrt,
                    bias=cst[:, 1:2], scale=1.0), inc="sc")
            p.wait("vector", "sc", sc_sd)
            for i in range(2):
                p.go("vector", lambda e, i=i: e.reciprocal(
                    svr[:, 10 + i:11 + i], svr[:, 8 + i:9 + i]), inc="vc")
            p.go("vector", lambda e: e.drain())
            for i in range(2):
                p.go("vector", lambda e, i=i: e.tensor_scalar(
                    svr[:, 12 + i:13 + i], svr[:, 10 + i:11 + i], 256.0, None,
                    ALU.mult), inc="vc")
            p.go("vector", lambda e: e.drain())
            for i in range(2):
                p.go("vector", lambda e, i=i: e.tensor_mul(
                    svr[:, 10 + i:11 + i], svr[:, 12 + i:13 + i],
                    svr[:, 4 + i:5 + i]), inc="vc")
            p.go("vector", lambda e: e.drain())
            for i in range(2):
                vc_rows = p.go("vector", lambda e, i=i: e.tensor_scalar(
                    svr[:, 14 + i:15 + i], svr[:, 10 + i:11 + i], -1.0, 2048.5,
                    ALU.mult, ALU.add), inc="vc")
            p.go("vector", lambda e: e.drain())
            p.wait("scalar", "vc", vc_rows)
            p.wait("scalar", "tn", tn_xnat_free)
            p.waits("scalar", apply_extra_waits)
            sc_ap = None
            for i in range(2):
                dstn = xna if i == 0 else xnb
                sc_ap = p.go("scalar", lambda e, d=dstn, i=i: e.activation(
                    d[:], gb[:, i * D:(i + 1) * D], AF.Identity,
                    bias=svr[:, 14 + i:15 + i],
                    scale=svr[:, 12 + i:13 + i]), inc="sc")
            ztiles = [(xna if i == 0 else xnb)[:, kd * 128:(kd + 1) * 128]
                      for i in range(2) for kd in range(KD)]
            dstz = lambda idx: hb[:, (idx % KD) * TP + (idx // KD) * 128:
                                  (idx % KD) * TP + (idx // KD) * 128 + 128]
            transpose_pass(ztiles, dstz,
                           [("sc", sc_ap)], [("vc", vc_gbhb_free)])
            p.wait("vector", "tn", p.cnt["tn"])
            vc_lnf = None
            for hh in range(2):
                zh = hb[:, hh * D:(hh + 1) * D]
                d_ = xna
                c_ = xnb
                p.go("vector", lambda e, zh=zh: e.tensor_copy(d_[:], zh),
                     inc="vc")
                for b in range(11, -1, -1):
                    p.go("vector", lambda e, b=b: e.tensor_scalar(
                        c_[:], d_[:], float(2 ** b), None, ALU.is_ge),
                        inc="vc")
                    p.go("vector", lambda e, b=b: e.scalar_tensor_tensor(
                        d_[:], c_[:], -float(2 ** b), d_[:], ALU.mult,
                        ALU.add), inc="vc")
                p.go("vector", lambda e, zh=zh: e.scalar_tensor_tensor(
                    zh, d_[:], -1.0, zh, ALU.mult, ALU.add), inc="vc")
                vc_lnf = p.go("vector", lambda e, zh=zh, hh=hh: e.tensor_scalar(
                    lx[:, hh * D:(hh + 1) * D], zh, -2048.0, None, ALU.add),
                    inc="vc")
            return {"vc_lnf": vc_lnf, "gd_x": gd_x}

        pro = emit_prologue(0, 0, 0, 0, [("wd", prep_end["wd"])])
        od_prev = 0

        for c in range(nch):
            r0 = c * TP

            xtiles = [gb[:, i * D + kd * 128:i * D + (kd + 1) * 128]
                      for i in range(2) for kd in range(KD)]
            dstx = lambda idx: xt[:, (idx % KD) * TP + (idx // KD) * 128:
                                  (idx % KD) * TP + (idx // KD) * 128 + 128]
            transpose_pass(xtiles, dstx,
                           [("gd", pro["gd_x"])] + gate["xt_free"], [])
            tn_xtr = p.cnt["tn"]
            vc_lnf = pro["vc_lnf"]
            first_ifg = True
            sc_f_done = vc_ifg_done = None
            ATTNMONO = True
            for s_ in range(KD):
                tn_w = {}
                if ATTNMONO:
                    slots3, wdn3 = ringA.submit([wis_d[s_, :, :],
                                                 wfs_d[s_, :, :],
                                                 wgs_d[s_, :, :]])
                for wj, (scr, psap, key) in enumerate(
                        ((wis_d, ps_i, "i"),
                         (wfs_d, ps_f, "f"),
                         (wgs_d, ps_g, "g"))):
                    if ATTNMONO:
                        slot, wdn = slots3[wj], wdn3
                    else:
                        slots, wdn = ringA.submit([scr[s_, :, :]])
                        slot = slots[0]
                    p.wait("tensor", "wd", wdn)
                    if first_ifg:
                        p.wait("tensor", "vc", vc_lnf)
                        first_ifg = False
                    p.waits("tensor", bfree[key])
                    for kd in range(KD):
                        p.go("tensor", lambda e, psap=psap, slot=slot, kd=kd,
                             st=(kd == 0), sp=(kd == KD - 1):
                             e.matmul(psap[:],
                                      slot[:, kd * 128:(kd + 1) * 128],
                                      lx[:, kd * TP:kd * TP + TP],
                                      start=st, stop=sp),
                             inc="tn" if kd == KD - 1 else None)
                    tn_w[key] = p.cnt["tn"]
                    if not ATTNMONO:
                        ringA.consumed(1, [("tn", p.cnt["tn"])])
                if ATTNMONO:
                    ringA.consumed(3, [("tn", p.cnt["tn"])])
                p.wait("vector", "tn", tn_w["i"])
                if s_ == 0:
                    p.waits("vector", gate["hb_free"] + [("tn", tn_xtr)])
                e1 = p.go("vector", lambda e, s_=s_: e.tensor_scalar(
                    hb[:, s_ * TP:(s_ + 1) * TP], ps_i[:],
                    sv_i[:, s_:s_ + 1], None, ALU.mult), inc="vc")
                bfree["i"] = [("vc", e1)]
                p.wait("scalar", "tn", tn_w["f"])
                if s_ == 0:
                    p.waits("scalar", [("vc", vc_lnf)])
                e2 = p.go("scalar", lambda e, s_=s_: e.activation(
                    zt[:, s_ * TP:(s_ + 1) * TP], ps_f[:], AF.Sigmoid,
                    scale=sv_f[:, s_:s_ + 1]), inc="sc")
                bfree["f"] = [("sc", e2)]
                par = s_ % 2
                p.wait("vector", "tn", tn_w["g"])
                e4 = p.go("vector", lambda e, s_=s_: e.tensor_scalar(
                    gb[:, s_ * TP:(s_ + 1) * TP], ps_g[:],
                    sv_g[:, s_:s_ + 1], None, ALU.mult), inc="vc")
                bfree["g"] = [("vc", e4)]
                p.wait("scalar", "vc", e4)
                p.wait("scalar", "vc", gsg_free[par])
                e3 = p.go("scalar", lambda e, par=par, s_=s_: e.activation(
                    gsg[par][:], gb[:, s_ * TP:(s_ + 1) * TP], AF.Sigmoid),
                    inc="sc")
                p.wait("vector", "sc", e3)
                vc_ifg_done = p.go("vector", lambda e, s_=s_, par=par:
                                   e.tensor_mul(
                                       gb[:, s_ * TP:(s_ + 1) * TP],
                                       gb[:, s_ * TP:(s_ + 1) * TP],
                                       gsg[par][:]), inc="vc")
                gsg_free[par] = vc_ifg_done
                sc_f_done = e2
            p.wait("vector", "sc", sc_f_done)
            for hh in range(2):
                tmp = xna if hh == 0 else xnb
                p.go("vector", lambda e, hh=hh, tmp=tmp: e.tensor_mul(
                    tmp[:], zt[:, hh * D:(hh + 1) * D],
                    hb[:, hh * D:(hh + 1) * D]), inc="vc")
                p.go("vector", lambda e, hh=hh, tmp=tmp: e.tensor_sub(
                    hb[:, hh * D:(hh + 1) * D], hb[:, hh * D:(hh + 1) * D],
                    tmp[:]), inc="vc")
            for s_ in range(KD):
                init = (0.0 if c % (SEQ // TP) == 0 else carry[:, s_:s_ + 1])
                p.go("vector", lambda e, s_=s_, init=init:
                     e.tensor_tensor_scan(
                         hb[:, s_ * TP:(s_ + 1) * TP],
                         zt[:, s_ * TP:(s_ + 1) * TP],
                         hb[:, s_ * TP:(s_ + 1) * TP],
                         init, ALU.mult, ALU.add), inc="vc")
            p.go("vector", lambda e: e.drain())
            for s_ in range(KD):
                vc_carry = p.go("vector", lambda e, s_=s_: e.tensor_copy(
                    carry[:, s_:s_ + 1],
                    hb[:, s_ * TP + TP - 1:s_ * TP + TP]), inc="vc")
            p.wait("scalar", "vc", vc_carry)
            p.wait("scalar", "od", od_prev)
            sq_sc = None
            for i in range(2):
                sq_sc = p.go("scalar", lambda e, i=i: e.activation(
                    (xna if i == 0 else xnb)[:], hb[:, i * D:(i + 1) * D],
                    AF.Square), inc="sc")
            p.wait("tensor", "sc", sq_sc)
            p.waits("tensor", bfree["sb"])
            for kd in range(KD):
                src = xna if kd < 8 else xnb
                tn_rms = p.go("tensor", lambda e, src=src, kd=kd: e.matmul(
                    ps_sb[0:1, 0:256], onc[:],
                    src[:, (kd % 8) * TP:(kd % 8 + 1) * TP],
                    start=(kd == 0), stop=(kd == KD - 1)),
                    inc="tn" if kd == KD - 1 else None)
            p.wait("scalar", "tn", tn_rms)
            sc_sd2 = p.go("scalar", lambda e: e.activation(
                rowb[0:1, 0:256], ps_sb[0:1, 0:256], AF.Sqrt,
                bias=cst[0:1, 1:2], scale=1.0 / D), inc="sc")
            p.wait("vector", "sc", sc_sd2)
            vc_rr = p.go("vector", lambda e: e.reciprocal(
                rowb[0:1, 256:512], rowb[0:1, 0:256]), inc="vc")
            p.wait("tensor", "vc", vc_rr)
            tn_bc = p.go("tensor", lambda e: e.matmul(
                ps_sb[:, 256:512], onr[0:1, :], rowb[0:1, 256:512],
                start=True, stop=True), inc="tn")
            p.wait("vector", "tn", tn_bc)
            vc_bc = p.go("vector", lambda e: e.tensor_copy(
                bc[:, 0:256], ps_sb[:, 256:512]), inc="vc")
            bfree["sb"] = [("vc", vc_bc)]
            p.wait("vector", "sc", sq_sc)
            for s_ in range(KD):
                p.go("vector", lambda e, s_=s_: e.tensor_mul(
                    hb[:, s_ * TP:(s_ + 1) * TP], hb[:, s_ * TP:(s_ + 1) * TP],
                    bc[:, 0:256]), inc="vc")
            p.go("vector", lambda e: e.tensor_mul(hb[:], hb[:], gb[:]),
                 inc="vc")
            for s_ in range(KD):
                p.go("vector", lambda e, s_=s_: e.tensor_scalar(
                    hb[:, s_ * TP:(s_ + 1) * TP], hb[:, s_ * TP:(s_ + 1) * TP],
                    sv_gn[:, s_:s_ + 1], None, ALU.mult), inc="vc")
            p.waits("vector", gate["h2_free"])
            p.go("vector", lambda e: e.tensor_copy(lnh[:, 0:KD * TP], hb[:]), inc="vc")
            vc_o = p.go("vector", lambda e: e.tensor_sub(lnl[:, 0:KD * TP], hb[:], lnh[:, 0:KD * TP]),
                        inc="vc")
            first_wo = True
            for os_ in range(KD):
                slots, wdn = ringB.submit([wos_d[os_, :, :]])
                par = os_ % 2
                wok = "mm0" if par == 0 else "mm1"
                psap = ps_mm[par]
                p.wait("tensor", "wd", wdn)
                if first_wo:
                    p.wait("tensor", "vc", vc_o)
                    first_wo = False
                p.waits("tensor", bfree[wok])
                nmm = 0
                for ke in range(KD):
                    for rhs in (lnh, lnl):
                        last = (ke == KD - 1 and rhs is lnl)
                        tn_wo = p.go("tensor", lambda e, psap=psap, ke=ke,
                                     rhs=rhs, st=(nmm == 0), sp=last,
                                     slot=slots[0]:
                                     e.matmul(psap[:],
                                              slot[:, ke * 128:(ke + 1) * 128],
                                              rhs[:, ke * TP:ke * TP + TP],
                                              start=st, stop=sp),
                                     inc="tn" if last else None)
                        nmm += 1
                ringB.consumed(1, [("tn", p.cnt["tn"])])
                p.wait("scalar", "tn", tn_wo)
                if os_ == 0:
                    p.wait("scalar", "vc", vc_carry)
                e4 = p.go("scalar", lambda e, os_=os_, psap=psap: e.activation(
                    zt[:, os_ * TP:(os_ + 1) * TP], psap[:], AF.Identity,
                    scale=256.0, bias=cst[:, 0:1]), inc="sc")
                bfree[wok] = [("sc", e4)]
            sc_wo = p.cnt["sc"]
            p.wait("vector", "sc", sc_wo)
            p.wait("vector", "od", od_prev)
            p.wait("vector", "tn", p.cnt["tn"])
            for hh in range(2):
                zh = zt[:, hh * D:(hh + 1) * D]
                d_ = xna
                c_ = xnb
                p.go("vector", lambda e, zh=zh: e.tensor_copy(d_[:], zh),
                     inc="vc")
                for b in range(12, -1, -1):
                    p.go("vector", lambda e, b=b: e.tensor_scalar(
                        c_[:], d_[:], float(2 ** b), None, ALU.is_ge), inc="vc")
                    p.go("vector", lambda e, b=b: e.scalar_tensor_tensor(
                        d_[:], c_[:], -float(2 ** b), d_[:], ALU.mult, ALU.add),
                        inc="vc")
                p.go("vector", lambda e, zh=zh: e.scalar_tensor_tensor(
                    zh, d_[:], -1.0, zh, ALU.mult, ALU.add), inc="vc")
                p.go("vector", lambda e, zh=zh: e.tensor_scalar(
                    zh, zh, 1.0 / 256.0, -16.0, ALU.mult, ALU.add), inc="vc")
                vc_y1 = p.go("vector", lambda e, zh=zh, hh=hh: e.tensor_add(
                    xt[:, hh * D:(hh + 1) * D], xt[:, hh * D:(hh + 1) * D],
                    zh), inc="vc")
            p.wait("scalar", "vc", vc_y1)
            sq2 = None
            for i in range(2):
                sq2 = p.go("scalar", lambda e, i=i: e.activation(
                    (xna if i == 0 else xnb)[:], xt[:, i * D:(i + 1) * D],
                    AF.Square), inc="sc")
            p.wait("tensor", "sc", sq2)
            p.waits("tensor", bfree["sb"])
            for kd in range(KD):
                p.go("tensor", lambda e, kd=kd: e.matmul(
                    ps_sb[0:1, 0:256], onc[:], xt[:, kd * TP:(kd + 1) * TP],
                    start=(kd == 0), stop=(kd == KD - 1)),
                    inc="tn" if kd == KD - 1 else None)
            for kd in range(KD):
                src = xna if kd < 8 else xnb
                tn_st2 = p.go("tensor", lambda e, src=src, kd=kd: e.matmul(
                    ps_sb[0:1, 256:512], onc[:],
                    src[:, (kd % 8) * TP:(kd % 8 + 1) * TP],
                    start=(kd == 0), stop=(kd == KD - 1)),
                    inc="tn" if kd == KD - 1 else None)
            p.wait("vector", "tn", tn_st2)
            p.go("vector", lambda e: e.tensor_scalar(
                rowb[0:1, 0:256], ps_sb[0:1, 0:256], 1.0 / D, None, ALU.mult),
                inc="vc")
            p.go("vector", lambda e: e.tensor_scalar(
                rowb[0:1, 256:512], ps_sb[0:1, 256:512], 1.0 / D, None,
                ALU.mult), inc="vc")
            p.go("vector", lambda e: e.drain())
            p.go("vector", lambda e: e.tensor_mul(
                rowb[0:1, 512:768], rowb[0:1, 0:256], rowb[0:1, 0:256]),
                inc="vc")
            p.go("vector", lambda e: e.drain())
            vc_v2 = p.go("vector", lambda e: e.tensor_sub(
                rowb[0:1, 256:512], rowb[0:1, 256:512], rowb[0:1, 512:768]),
                inc="vc")
            p.wait("scalar", "vc", vc_v2)
            sc_sd3 = p.go("scalar", lambda e: e.activation(
                rowb[0:1, 512:768], rowb[0:1, 256:512], AF.Sqrt,
                bias=cst[0:1, 1:2], scale=1.0), inc="sc")
            p.wait("vector", "sc", sc_sd3)
            p.go("vector", lambda e: e.reciprocal(
                rowb[0:1, 256:512], rowb[0:1, 512:768]), inc="vc")
            p.go("vector", lambda e: e.drain())
            vc_mi = p.go("vector", lambda e: e.tensor_mul(
                rowb[0:1, 512:768], rowb[0:1, 0:256], rowb[0:1, 256:512]),
                inc="vc")
            p.wait("tensor", "vc", vc_mi)
            tn_bc2 = p.go("tensor", lambda e: e.matmul(
                ps_sb[:, 0:512], onr[0:1, :], rowb[0:1, 256:768],
                start=True, stop=True), inc="tn")
            p.wait("vector", "tn", tn_bc2)
            vc_bc2 = p.go("vector", lambda e: e.tensor_copy(
                bc[:, 0:512], ps_sb[:, 0:512]), inc="vc")
            bfree["sb"] = [("vc", vc_bc2)]
            p.wait("vector", "tn", p.cnt["tn"])
            for s_ in range(KD):
                p.go("vector", lambda e, s_=s_: e.tensor_mul(
                    zt[:, s_ * TP:(s_ + 1) * TP], xt[:, s_ * TP:(s_ + 1) * TP],
                    bc[:, 0:256]), inc="vc")
            for s_ in range(KD):
                p.go("vector", lambda e, s_=s_: e.tensor_sub(
                    zt[:, s_ * TP:(s_ + 1) * TP], zt[:, s_ * TP:(s_ + 1) * TP],
                    bc[:, 256:512]), inc="vc")
            p.go("vector", lambda e: e.tensor_copy(lnh[:, 0:KD * TP], zt[:]), inc="vc")
            vc_ln2 = p.go("vector", lambda e: e.tensor_sub(
                lnl[:, 0:KD * TP], zt[:], lnh[:, 0:KD * TP]), inc="vc")

            def ffn_stage(n_out, kk, hi_scr, lo_scr, rhs_h, rhs_l, dst_h,
                          dst_l, bias_sb, func, first_wait, h_free_gate,
                          resid=False):
                prev = None
                last_mm = None
                first = True
                sc_last = None
                vc_last = None
                t1_free = [0, 0]
                for s_ in range(n_out):
                    par = s_ % 2
                    fk = "mm0" if par == 0 else "mm1"
                    psap = ps_mm[par]
                    nhalf = kk // KD
                    if False:
                        nmm = 0
                        total = kk * 3
                        if nhalf == 1:
                            srcs = [hi_scr[s_, :, :], lo_scr[s_, :, :]]
                        else:
                            srcs = [hi_scr[s_, :, 0:D], hi_scr[s_, :, D:2 * D],
                                    lo_scr[s_, :, 0:D], lo_scr[s_, :, D:2 * D]]
                        slots, wdn = ringB.submit(srcs)
                        p.wait("tensor", "wd", wdn)
                        if first:
                            p.waits("tensor", first_wait)
                            first = False
                        p.waits("tensor", bfree[fk])

                        def lhs_tile(k):
                            if nhalf == 1:
                                return slots[0][:, k * 128:(k + 1) * 128], \
                                    slots[1][:, k * 128:(k + 1) * 128]
                            if k < 16:
                                return slots[0][:, k * 128:(k + 1) * 128], \
                                    slots[2][:, k * 128:(k + 1) * 128]
                            return slots[1][:, (k - 16) * 128:(k - 15) * 128], \
                                slots[3][:, (k - 16) * 128:(k - 15) * 128]
                        for combo in range(3):
                            for k in range(kk):
                                hi_t, lo_t = lhs_tile(k)
                                lhs = hi_t if combo < 2 else lo_t
                                rhs = rhs_h if combo in (0, 2) else rhs_l
                                last = (nmm == total - 1)
                                last_mm = p.go(
                                    "tensor", lambda e, psap=psap, lhs=lhs,
                                    rhs=rhs, k=k, st=(nmm == 0), sp=last:
                                    e.matmul(psap[:], lhs,
                                             rhs[:, k * TP:k * TP + TP],
                                             start=st, stop=sp),
                                    inc="tn" if last else None)
                                nmm += 1
                        ringB.consumed(len(srcs), [("tn", p.cnt["tn"])])
                    else:
                        nmm = 0
                        total = kk * 3
                        for half in range(nhalf):
                            srcs = [hi_scr[s_, :, half * D:(half + 1) * D],
                                    lo_scr[s_, :, half * D:(half + 1) * D]]
                            slots, wdn = ringB.submit(srcs)
                            p.wait("tensor", "wd", wdn)
                            if first:
                                p.waits("tensor", first_wait)
                                first = False
                            if half == 0:
                                p.waits("tensor", bfree[fk])
                            half_elems = KD * 3
                            for combo in range(3):
                                for kq in range(KD):
                                    k = half * KD + kq
                                    lhs_sl = slots[0] if combo < 2 else slots[1]
                                    lhs = lhs_sl[:, kq * 128:(kq + 1) * 128]
                                    rhs = rhs_h if combo in (0, 2) else rhs_l
                                    last = (nmm == total - 1)
                                    hend = (nmm % half_elems == half_elems - 1)
                                    last_mm = p.go(
                                        "tensor", lambda e, psap=psap, lhs=lhs,
                                        rhs=rhs, k=k, st=(nmm == 0), sp=last:
                                        e.matmul(psap[:], lhs,
                                                 rhs[:, k * TP:k * TP + TP],
                                                 start=st, stop=sp),
                                        inc="tn" if (last or hend) else None)
                                    nmm += 1
                            ringB.consumed(2, [("tn", p.cnt["tn"])])
                    p.wait("scalar", "tn", last_mm)
                    p.wait("scalar", "vc", t1_free[s_ % 2])
                    sc_last = p.go("scalar", lambda e, s_=s_, psap=psap:
                                   e.activation(
                                       t1[s_ % 2][:], psap[:], func,
                                       bias=bias_sb[:, s_:s_ + 1],
                                       scale=1.0), inc="sc")
                    bfree[fk] = [("sc", sc_last)]
                    p.wait("vector", "sc", sc_last)
                    if s_ == 0:
                        p.waits("vector", h_free_gate)
                    if resid:
                        vc_last = p.go("vector", lambda e, s_=s_:
                                       e.tensor_add(
                                           xt[:, s_ * TP:(s_ + 1) * TP],
                                           xt[:, s_ * TP:(s_ + 1) * TP],
                                           t1[s_ % 2][:]), inc="vc")
                        t1_free[s_ % 2] = vc_last
                    else:
                        vc_last = p.go("vector", lambda e, s_=s_: e.tensor_copy(
                            dst_h[:, s_ * TP:(s_ + 1) * TP], t1[s_ % 2][:]),
                            inc="vc")
                        if prev is not None:
                            ps_, buf_ = prev
                            vc_last = p.go(
                                "vector", lambda e, ps_=ps_, buf_=buf_:
                                e.tensor_sub(
                                    dst_l[:, ps_ * TP:(ps_ + 1) * TP],
                                    buf_[:],
                                    dst_h[:, ps_ * TP:(ps_ + 1) * TP]),
                                inc="vc")
                            t1_free[ps_ % 2] = vc_last
                        prev = (s_, t1[s_ % 2])
                if not resid and prev is not None:
                    ps_, buf_ = prev
                    p.go("vector", lambda e: e.drain())
                    vc_last = p.go("vector", lambda e, ps_=ps_, buf_=buf_:
                                   e.tensor_sub(
                                       dst_l[:, ps_ * TP:(ps_ + 1) * TP],
                                       buf_[:],
                                       dst_h[:, ps_ * TP:(ps_ + 1) * TP]),
                                   inc="vc")
                return last_mm, sc_last, vc_last

            mm1, _, vc_h1 = ffn_stage(
                KH, KD, w1h_d, w1l_d, lnh, lnl, h1h, h1l, b1s, AF.Relu,
                [("vc", vc_ln2)], gate["h1_free"])
            if c + 1 < nch:
                pro = emit_prologue(c + 1, vc_o, vc_ln2, tn_st2, [])
            mm2, _, vc_h2 = ffn_stage(
                KH, KH, w2h_d, w2l_d, h1h, h1l, h2h, h2l, b2s, AF.Relu,
                [("vc", vc_h1)], gate["h2_free"])
            h1_free_next = [("tn", p.cnt["tn"])]
            mm3, sc3, vc_y2 = ffn_stage(
                KD, KH, w3h_d, w3l_d, h2h, h2l, None, None, b3s, AF.Identity,
                [("vc", vc_h2)], [], resid=True)
            h2_free_next = [("tn", p.cnt["tn"])]

            od_last = None
            for i in range(2):
                otiles = [xt[:, os_ * TP + i * 128:os_ * TP + (i + 1) * 128]
                          for os_ in range(KD)]
                dsto = lambda idx, i=i: (xna if i == 0 else xnb)[
                    :, idx * 128:(idx + 1) * 128]
                ev = transpose_pass(otiles, dsto,
                                    [("vc", vc_y2)], [("tn", tn_st2)])
                p.wait("gpsimd", "vc", ev)
                od_last = p.go("gpsimd", lambda e, i=i, r0=r0: e.dma_start(
                    out=y_d[r0 + i * 128:r0 + (i + 1) * 128, :],
                    in_=(xna if i == 0 else xnb)[:]), inc="od", n=16)

            od_prev = od_last
            gate["xt_free"] = [("tn", p.cnt["tn"])]
            gate["hb_free"] = [("vc", p.cnt["vc"])]
            gate["h1_free"] = h1_free_next
            gate["h2_free"] = h2_free_next

        p.wait("gpsimd", "od", p.cnt["od"])
        p.wait("gpsimd", "wd", p.cnt["wd"])

        with nc.Block() as block:
            @block.tensor
            def _(eng):
                for fn in p.streams["tensor"]:
                    fn(eng)

            @block.vector
            def _(eng):
                for fn in p.streams["vector"]:
                    fn(eng)

            @block.scalar
            def _(eng):
                for fn in p.streams["scalar"]:
                    fn(eng)

            @block.gpsimd
            def _(eng):
                for fn in p.streams["gpsimd"]:
                    fn(eng)

            @block.sync
            def _(eng):
                for fn in p.streams["sync"]:
                    fn(eng)
    return nc


def _pack_vec(v, k):
    return np.ascontiguousarray(np.asarray(v, dtype=np.float32)
                                .reshape(k, 128).T)


def _tile_in(w, ks, km):
    w = np.asarray(w, dtype=np.float32)
    return np.ascontiguousarray(
        w.reshape(ks, 128, km, 128).transpose(0, 3, 2, 1)
        .reshape(ks, 128, km * 128))


def _tile_nat(w, ks, km):
    w = np.asarray(w, dtype=np.float32)
    return np.ascontiguousarray(
        w.reshape(km, 128, ks, 128).transpose(2, 1, 0, 3)
        .reshape(ks, 128, km * 128))


def kernel(**inp):
    x = np.asarray(inp["L0_attn_hidden_states"], dtype=np.float32)
    base = {
        "wit": _tile_nat(inp["L0_attn_w_i"], KD, KD),
        "wft": _tile_nat(inp["L0_attn_w_f"], KD, KD),
        "wgt": _tile_nat(inp["L0_attn_w_g"], KD, KD),
        "wot": _tile_in(inp["L0_attn_w_o"], KD, KD),
        "w1t": _tile_in(inp["L0_ffn_W1"], KH, KD),
        "w2t": _tile_in(inp["L0_ffn_W2"], KH, KH),
        "w3t": _tile_in(inp["L0_ffn_W3"], KD, KH),
        "svi": _pack_vec(inp["L0_attn_s_i"], KD),
        "svf": _pack_vec(inp["L0_attn_s_f"], KD),
        "svg": _pack_vec(inp["L0_attn_s_g"], KD),
        "svo": _pack_vec(inp["L0_attn_s_o"], KD),
        "gnw": _pack_vec(inp["L0_attn_g_norm_w"], KD),
        "b1m": _pack_vec(inp["L0_ffn_b1"], KH),
        "b2m": _pack_vec(inp["L0_ffn_b2"], KH),
        "b3m": _pack_vec(inp["L0_ffn_b3"], KD),
        "ident": np.eye(128, dtype=np.float32),
        "onec": np.ones((128, 1), dtype=np.float32),
        "oner": np.ones((1, 128), dtype=np.float32),
    }
    nc = build()
    in_maps = []
    for c in range(8):
        m = dict(base)
        m["x"] = np.ascontiguousarray(x[2 * c:2 * c + 2].reshape(NT, D))
        in_maps.append(m)
    res = run_bass_kernel_spmd(nc, in_maps, core_ids=list(range(8)))
    y = np.concatenate(
        [np.asarray(r["y"]).reshape(2, SEQ, D) for r in res.results], axis=0)
    return y


# version 09 (final)
# speedup vs baseline: 1.3078x; 1.0381x over previous
import os
import numpy as np
import concourse.bass as bass
import concourse.mybir as mybir
from concourse.bass_utils import run_bass_kernel_spmd
from contextlib import ExitStack

F32 = mybir.dt.float32
BF16 = mybir.dt.bfloat16
FP16 = mybir.dt.float16
AF = mybir.ActivationFunctionType
ALU = mybir.AluOpType
AX = mybir.AxisListType

D = 2048
H = 4096
TP = 256
SEQ = 2048
NT = 2 * SEQ
NCH = NT // TP
KD = D // 128
KH = H // 128
EPS = 1e-5
NSA = 6
NSB = 5

ENGS = ("tensor", "vector", "scalar", "gpsimd", "sync")


class Prog:
    def __init__(self, nc, es):
        self.nc = nc
        self.streams = {e: [] for e in ENGS}
        self.sems = {}
        self.cnt = {}
        for name in ("wd", "gd", "od", "tn", "vc", "sc"):
            self.sems[name] = es.enter_context(nc.semaphore(name))
            self.cnt[name] = 0

    def wait(self, eng, sem, val):
        if val <= 0:
            return
        s = self.sems[sem]
        self.streams[eng].append(lambda e, s=s, v=val: e.wait_ge(s, v))

    def waits(self, eng, lst):
        for sem, val in lst:
            self.wait(eng, sem, val)

    def go(self, eng, fn, inc=None, n=1):
        if inc is None:
            self.streams[eng].append(fn)
            return None
        self.cnt[inc] += n
        s = self.sems[inc]

        def run(e, fn=fn, s=s, n=n):
            fn(e).then_inc(s, n)
        self.streams[eng].append(run)
        return self.cnt[inc]


def build(nch=NCH):
    nc = bass.Bass("TRN2", target_bir_lowering=False, debug=False,
                   detect_race_conditions=False)
    x_d = nc.dram_tensor("x", [NT, D], F32, kind="ExternalInput").ap()
    y_d = nc.dram_tensor("y", [NT, D], F32, kind="ExternalOutput").ap()
    wit_d = nc.dram_tensor("wit", [KD, 128, D], F32, kind="ExternalInput").ap()
    wft_d = nc.dram_tensor("wft", [KD, 128, D], F32, kind="ExternalInput").ap()
    wgt_d = nc.dram_tensor("wgt", [KD, 128, D], F32, kind="ExternalInput").ap()
    wot_d = nc.dram_tensor("wot", [KD, 128, D], F32, kind="ExternalInput").ap()
    w1t_d = nc.dram_tensor("w1t", [KH, 128, D], F32, kind="ExternalInput").ap()
    w2t_d = nc.dram_tensor("w2t", [KH, 128, H], F32, kind="ExternalInput").ap()
    w3t_d = nc.dram_tensor("w3t", [KD, 128, H], F32, kind="ExternalInput").ap()
    svi_d = nc.dram_tensor("svi", [128, KD], F32, kind="ExternalInput").ap()
    svf_d = nc.dram_tensor("svf", [128, KD], F32, kind="ExternalInput").ap()
    svg_d = nc.dram_tensor("svg", [128, KD], F32, kind="ExternalInput").ap()
    svo_d = nc.dram_tensor("svo", [128, KD], F32, kind="ExternalInput").ap()
    gnw_d = nc.dram_tensor("gnw", [128, KD], F32, kind="ExternalInput").ap()
    b1_d = nc.dram_tensor("b1m", [128, KH], F32, kind="ExternalInput").ap()
    b2_d = nc.dram_tensor("b2m", [128, KH], F32, kind="ExternalInput").ap()
    b3_d = nc.dram_tensor("b3m", [128, KD], F32, kind="ExternalInput").ap()
    id_d = nc.dram_tensor("ident", [128, 128], F32, kind="ExternalInput").ap()
    onc_d = nc.dram_tensor("onec", [128, 1], F32, kind="ExternalInput").ap()
    onr_d = nc.dram_tensor("oner", [1, 128], F32, kind="ExternalInput").ap()
    wis_d = nc.dram_tensor("wis", [KD, 128, D], FP16).ap()
    wfs_d = nc.dram_tensor("wfs", [KD, 128, D], FP16).ap()
    wgs_d = nc.dram_tensor("wgs", [KD, 128, D], FP16).ap()
    wos_d = nc.dram_tensor("wos", [KD, 128, D], BF16).ap()
    w1h_d = nc.dram_tensor("w1h", [KH, 128, D], BF16).ap()
    w1l_d = nc.dram_tensor("w1l", [KH, 128, D], BF16).ap()
    w2h_d = nc.dram_tensor("w2h", [KH, 128, H], BF16).ap()
    w2l_d = nc.dram_tensor("w2l", [KH, 128, H], BF16).ap()
    w3h_d = nc.dram_tensor("w3h", [KD, 128, H], BF16).ap()
    w3l_d = nc.dram_tensor("w3l", [KD, 128, H], BF16).ap()

    es = ExitStack()
    with es:
        sb = lambda name, shape, dt=F32: es.enter_context(
            nc.sbuf_tensor(name, shape, dt))
        xna = sb("xna", [128, D])
        xnb = sb("xnb", [128, D])
        xt = sb("xt", [128, KD * TP])
        zt = sb("zt", [128, KD * TP])
        lx = sb("lx", [128, KD * TP], FP16)
        hb = sb("hb", [128, KD * TP])
        gb = sb("gb", [128, KD * TP])
        h1h = sb("h1h", [128, KH * TP], BF16)
        h1l = sb("h1l", [128, KH * TP], BF16)
        h2h = sb("h2h", [128, KH * TP], BF16)
        h2l = sb("h2l", [128, KH * TP], BF16)
        lnh = h2h
        lnl = h2l
        wpa = [sb(f"wpa{j}", [128, D], FP16) for j in range(NSA)]
        wpb = [sb(f"wpb{j}", [128, D], BF16) for j in range(NSB)]
        sv_i = sb("sv_i", [128, KD])
        sv_f = sb("sv_f", [128, KD])
        sv_g = sb("sv_g", [128, KD])
        sv_o = sb("sv_o", [128, KD])
        sv_gn = sb("sv_gn", [128, KD])
        b1s = sb("b1s", [128, KH])
        b2s = sb("b2s", [128, KH])
        b3s = sb("b3s", [128, KD])
        id32 = sb("id32", [128, 128])
        onc = sb("onc", [128, 1])
        onr = sb("onr", [1, 128])
        cst = sb("cst", [128, 2])
        svr = sb("svr", [128, 16])
        rowb = sb("rowb", [1, 768])
        bc = sb("bc", [128, 512])
        t1 = [sb(f"t1{j}", [128, TP]) for j in range(2)]
        gsg = t1
        carry = sb("carry", [128, KD])
        psb = lambda name: es.enter_context(
            nc.psum_tensor(name, [128, 512], F32))
        psh = lambda name: es.enter_context(
            nc.psum_tensor(name, [128, 256], F32))
        ps_i = psh("ps_i")
        ps_f = psh("ps_f")
        ps_g = psh("ps_g")
        ps_mm = [psh("ps_mm0"), psh("ps_mm1")]
        ps_trA = psb("ps_trA")
        ps_trB = psb("ps_trB")
        ps_sb = psb("ps_sb")
        tr_banks = [ps_trA, ps_trB]

        p = Prog(nc, es)

        for dst, src in ((sv_i, svi_d), (sv_f, svf_d), (sv_g, svg_d),
                         (sv_o, svo_d), (sv_gn, gnw_d), (b1s, b1_d),
                         (b2s, b2_d), (b3s, b3_d), (id32, id_d),
                         (onc, onc_d), (onr, onr_d)):
            p.go("sync", lambda e, d=dst, s=src: e.dma_start(out=d[:], in_=s),
                 inc="wd", n=16)
        wd_small = p.cnt["wd"]
        p.wait("vector", "wd", wd_small)
        p.go("vector", lambda e: e.memset(cst[:, 0:1], 4096.5), inc="vc")
        p.go("vector", lambda e: e.memset(cst[:, 1:2], EPS), inc="vc")
        p.go("vector", lambda e: e.tensor_scalar(
            sv_i[:], sv_i[:], 1.0 / 256.0, None, ALU.mult), inc="vc")
        p.go("vector", lambda e: e.tensor_scalar(
            sv_f[:], sv_f[:], 1.0 / 256.0, None, ALU.mult), inc="vc")
        p.go("vector", lambda e: e.tensor_scalar(
            sv_g[:], sv_g[:], 1.0 / 256.0, None, ALU.mult), inc="vc")
        p.go("vector", lambda e: e.tensor_mul(sv_gn[:], sv_gn[:], sv_o[:]),
             inc="vc")
        p.go("vector", lambda e: e.drain())

        pbuf = [xna, xnb]
        pbuf_rd = [0, 0]
        slot_rd = {}

        def prep_cast(src_t, n_slab, width, hi_scr, lo_scr, hi_slot, lo_slot):
            for s_ in range(n_slab):
                for half in range(width // D):
                    j = (s_ * (width // D) + half) % 2
                    p.wait("sync", "vc", pbuf_rd[j])
                    ld = p.go("sync", lambda e, j=j, s_=s_, half=half:
                              e.dma_start(
                                  out=pbuf[j][:],
                                  in_=src_t[s_, :, half * D:(half + 1) * D]),
                              inc="wd", n=16)
                    p.wait("vector", "wd", ld)
                    p.wait("vector", "wd", slot_rd.get(id(hi_slot), 0))
                    cv = p.go("vector", lambda e, j=j, sl=hi_slot:
                              e.tensor_copy(sl[:], pbuf[j][:]), inc="vc")
                    if lo_scr is not None:
                        p.wait("vector", "wd", slot_rd.get(id(lo_slot), 0))
                        cv = p.go("vector", lambda e, j=j, hsl=hi_slot,
                                  lsl=lo_slot:
                                  e.tensor_sub(lsl[:], pbuf[j][:], hsl[:]),
                                  inc="vc")
                    pbuf_rd[j] = cv
                    p.wait("sync", "vc", cv)
                    for sl, scr in ([(hi_slot, hi_scr)] +
                                    ([(lo_slot, lo_scr)] if lo_scr is not None
                                     else [])):
                        slot_rd[id(sl)] = p.go(
                            "sync", lambda e, sl=sl, scr=scr, s_=s_, half=half:
                            e.dma_start(
                                out=scr[s_, :, half * D:(half + 1) * D],
                                in_=sl[:]), inc="wd", n=16)

        prep_cast(wit_d, KD, D, wis_d, None, wpa[0], None)
        prep_cast(wft_d, KD, D, wfs_d, None, wpa[1], None)
        prep_cast(wgt_d, KD, D, wgs_d, None, wpa[2], None)
        prep_cast(wot_d, KD, D, wos_d, None, wpb[0], None)
        prep_cast(w1t_d, KH, D, w1h_d, w1l_d, wpb[1], wpb[2])
        prep_cast(w2t_d, KH, H, w2h_d, w2l_d, wpb[3], wpb[4])
        prep_cast(w3t_d, KD, H, w3h_d, w3l_d, wpb[3], wpb[1])

        prep_end = {s: p.cnt[s] for s in ("wd", "vc", "tn", "sc")}
        init_xn_free = [("vc", prep_end["vc"])]

        class Ring:
            def __init__(self, slots, init_waits):
                self.slots = slots
                self.n = len(slots)
                self.load_idx = 0
                self.freed = []
                self.init_waits = init_waits

            def submit(self, srcs):
                out = []
                for src in srcs:
                    slot = self.slots[self.load_idx % self.n]
                    if self.load_idx >= self.n:
                        p.waits("sync", self.freed[self.load_idx - self.n])
                    else:
                        p.waits("sync", self.init_waits)
                    p.go("sync", lambda e, d=slot, s=src:
                         e.dma_start(out=d[:], in_=s), inc="wd", n=16)
                    out.append(slot)
                    self.load_idx += 1
                return out, p.cnt["wd"]

            def consumed(self, n, waits):
                for _ in range(n):
                    self.freed.append(list(waits))

        ringA = Ring(wpa, [("wd", prep_end["wd"])])
        ringB = Ring(wpb, [("wd", prep_end["wd"])])

        bfree = {"i": [], "f": [], "g": [], "mm0": [], "mm1": [],
                 "trA": [], "trB": [], "sb": []}

        gate = {
            "xn_free": list(init_xn_free),
            "xt_free": [],
            "lx_free": [],
            "lnf_free": [],
            "hb_free": [],
            "gb_free": [],
            "zt_free": [],
            "h1_free": [],
            "h2_free": [],
        }

        gsg_free = [0, 0]

        def transpose_pass(src_tiles, dst_fn, waits_tensor, waits_evac,
                           evac_eng="scalar"):
            evac_sem = "sc" if evac_eng == "scalar" else "vc"
            n = len(src_tiles)
            last_ev = None
            for r in range((n + 3) // 4):
                bank_i = r % 2
                bank = tr_banks[bank_i]
                key = "trA" if bank_i == 0 else "trB"
                p.waits("tensor", bfree[key])
                if r == 0:
                    p.waits("tensor", waits_tensor)
                tt = None
                for j in range(4):
                    idx = r * 4 + j
                    if idx >= n:
                        break
                    tt = p.go("tensor", lambda e, b=bank, j=j, s=src_tiles[idx]:
                              e.transpose(b[:, j * 128:(j + 1) * 128], s,
                                          id32[:]), inc="tn")
                p.wait(evac_eng, "tn", tt)
                if r == 0:
                    p.waits(evac_eng, waits_evac)
                ev = None
                for j in range(4):
                    idx = r * 4 + j
                    if idx >= n:
                        break
                    if evac_eng == "scalar":
                        ev = p.go("scalar", lambda e, b=bank, j=j,
                                  d=dst_fn(idx):
                                  e.copy(d, b[:, j * 128:(j + 1) * 128]),
                                  inc="sc")
                    else:
                        ev = p.go("vector", lambda e, b=bank, j=j,
                                  d=dst_fn(idx):
                                  e.tensor_copy(d, b[:, j * 128:(j + 1) * 128]),
                                  inc="vc")
                bfree[key] = [(evac_sem, ev)]
                last_ev = ev
            return last_ev

        bgq = []

        def bg_pump(k):
            n = min(k, len(bgq))
            for _ in range(n):
                bgq.pop(0)()

        def emit_prologue(c, vc_gbhb_free, vc_zt_free, tn_xnat_free,
                          apply_extra_waits):
            r0 = c * TP
            p.wait("gpsimd", "vc", vc_gbhb_free)
            gd_x = None
            for i in range(2):
                gd_x = p.go("gpsimd", lambda e, i=i, r0=r0:
                            e.dma_start(out=gb[:, i * D:(i + 1) * D],
                                        in_=x_d[r0 + i * 128:r0 + (i + 1) * 128, :]),
                            inc="gd", n=16)
            p.wait("vector", "gd", gd_x)
            for i in range(2):
                p.go("vector", lambda e, i=i:
                     e.reduce_sum(svr[:, i:i + 1], gb[:, i * D:(i + 1) * D],
                                  axis=AX.X), inc="vc")
            p.waits("scalar", [("gd", gd_x), ("vc", vc_zt_free)])
            sc_sq = None
            for i in range(2):
                sc_sq = p.go("scalar", lambda e, i=i:
                             e.activation(zt[:, i * D:(i + 1) * D],
                                          gb[:, i * D:(i + 1) * D],
                                          AF.Square), inc="sc")
            p.wait("vector", "sc", sc_sq)
            for i in range(2):
                p.go("vector", lambda e, i=i:
                     e.reduce_sum(svr[:, 2 + i:3 + i], zt[:, i * D:(i + 1) * D],
                                  axis=AX.X), inc="vc")
            p.go("vector", lambda e: e.drain())
            for i in range(2):
                p.go("vector", lambda e, i=i: e.tensor_scalar(
                    svr[:, 4 + i:5 + i], svr[:, i:i + 1], 1.0 / D, None,
                    ALU.mult), inc="vc")
            p.go("vector", lambda e: e.drain())
            for i in range(2):
                p.go("vector", lambda e, i=i: e.tensor_scalar(
                    svr[:, 6 + i:7 + i], svr[:, 2 + i:3 + i], 1.0 / D, None,
                    ALU.mult), inc="vc")
            p.go("vector", lambda e: e.drain())
            for i in range(2):
                p.go("vector", lambda e, i=i: e.tensor_mul(
                    svr[:, 8 + i:9 + i], svr[:, 4 + i:5 + i],
                    svr[:, 4 + i:5 + i]), inc="vc")
            p.go("vector", lambda e: e.drain())
            for i in range(2):
                vc_var = p.go("vector", lambda e, i=i: e.tensor_sub(
                    svr[:, 6 + i:7 + i], svr[:, 6 + i:7 + i],
                    svr[:, 8 + i:9 + i]), inc="vc")
            p.go("vector", lambda e: e.drain())
            p.wait("scalar", "vc", vc_var)
            for i in range(2):
                sc_sd = p.go("scalar", lambda e, i=i: e.activation(
                    svr[:, 8 + i:9 + i], svr[:, 6 + i:7 + i], AF.Sqrt,
                    bias=cst[:, 1:2], scale=1.0), inc="sc")
            p.wait("vector", "sc", sc_sd)
            for i in range(2):
                p.go("vector", lambda e, i=i: e.reciprocal(
                    svr[:, 10 + i:11 + i], svr[:, 8 + i:9 + i]), inc="vc")
            p.go("vector", lambda e: e.drain())
            for i in range(2):
                p.go("vector", lambda e, i=i: e.tensor_scalar(
                    svr[:, 12 + i:13 + i], svr[:, 10 + i:11 + i], 256.0, None,
                    ALU.mult), inc="vc")
            p.go("vector", lambda e: e.drain())
            for i in range(2):
                p.go("vector", lambda e, i=i: e.tensor_mul(
                    svr[:, 10 + i:11 + i], svr[:, 12 + i:13 + i],
                    svr[:, 4 + i:5 + i]), inc="vc")
            p.go("vector", lambda e: e.drain())
            for i in range(2):
                vc_rows = p.go("vector", lambda e, i=i: e.tensor_scalar(
                    svr[:, 14 + i:15 + i], svr[:, 10 + i:11 + i], -1.0, 2048.5,
                    ALU.mult, ALU.add), inc="vc")
            p.go("vector", lambda e: e.drain())
            p.wait("scalar", "vc", vc_rows)
            p.wait("scalar", "tn", tn_xnat_free)
            p.waits("scalar", apply_extra_waits)
            sc_ap = None
            for i in range(2):
                dstn = xna if i == 0 else xnb
                sc_ap = p.go("scalar", lambda e, d=dstn, i=i: e.activation(
                    d[:], gb[:, i * D:(i + 1) * D], AF.Identity,
                    bias=svr[:, 14 + i:15 + i],
                    scale=svr[:, 12 + i:13 + i]), inc="sc")
            ztiles = [(xna if i == 0 else xnb)[:, kd * 128:(kd + 1) * 128]
                      for i in range(2) for kd in range(KD)]
            dstz = lambda idx: hb[:, (idx % KD) * TP + (idx // KD) * 128:
                                  (idx % KD) * TP + (idx // KD) * 128 + 128]
            sc_ztr = transpose_pass(ztiles, dstz,
                                    [("sc", sc_ap)], [("vc", vc_gbhb_free)])
            tn_snap = p.cnt["tn"]
            state = {"vc_lnf": None, "gd_x": gd_x, "first": True}

            def bg(fn):
                def run(fn=fn):
                    if state["first"]:
                        p.wait("vector", "sc", sc_ztr)
                        p.wait("vector", "tn", tn_snap)
                        state["first"] = False
                    state["vc_lnf"] = p.go("vector", fn, inc="vc")
                bgq.append(run)

            for hh in range(2):
                zh = hb[:, hh * D:(hh + 1) * D]
                d_ = xna
                c_ = xnb
                bg(lambda e, zh=zh: e.tensor_copy(d_[:], zh))
                for b in range(11, -1, -1):
                    bg(lambda e, b=b: e.tensor_scalar(
                        c_[:], d_[:], float(2 ** b), None, ALU.is_ge))
                    bg(lambda e, b=b: e.scalar_tensor_tensor(
                        d_[:], c_[:], -float(2 ** b), d_[:], ALU.mult,
                        ALU.add))
                bg(lambda e, zh=zh: e.scalar_tensor_tensor(
                    zh, d_[:], -1.0, zh, ALU.mult, ALU.add))
                bg(lambda e, zh=zh, hh=hh: e.tensor_scalar(
                    lx[:, hh * D:(hh + 1) * D], zh, -2048.0, None, ALU.add))
            return state

        pro = emit_prologue(0, 0, 0, 0, [("wd", prep_end["wd"])])
        bg_pump(len(bgq))
        od_prev = 0

        for c in range(nch):
            r0 = c * TP

            xtiles = [gb[:, i * D + kd * 128:i * D + (kd + 1) * 128]
                      for i in range(2) for kd in range(KD)]
            dstx = lambda idx: xt[:, (idx % KD) * TP + (idx // KD) * 128:
                                  (idx % KD) * TP + (idx // KD) * 128 + 128]
            transpose_pass(xtiles, dstx,
                           [("gd", pro["gd_x"])] + gate["xt_free"], [])
            tn_xtr = p.cnt["tn"]
            vc_lnf = pro["vc_lnf"]
            first_ifg = True
            sc_f_done = vc_ifg_done = None
            ATTNMONO = True
            for s_ in range(KD):
                tn_w = {}
                if ATTNMONO:
                    slots3, wdn3 = ringA.submit([wis_d[s_, :, :],
                                                 wfs_d[s_, :, :],
                                                 wgs_d[s_, :, :]])
                for wj, (scr, psap, key) in enumerate(
                        ((wis_d, ps_i, "i"),
                         (wfs_d, ps_f, "f"),
                         (wgs_d, ps_g, "g"))):
                    if ATTNMONO:
                        slot, wdn = slots3[wj], wdn3
                    else:
                        slots, wdn = ringA.submit([scr[s_, :, :]])
                        slot = slots[0]
                    p.wait("tensor", "wd", wdn)
                    if first_ifg:
                        p.wait("tensor", "vc", vc_lnf)
                        first_ifg = False
                    p.waits("tensor", bfree[key])
                    for kd in range(KD):
                        p.go("tensor", lambda e, psap=psap, slot=slot, kd=kd,
                             st=(kd == 0), sp=(kd == KD - 1):
                             e.matmul(psap[:],
                                      slot[:, kd * 128:(kd + 1) * 128],
                                      lx[:, kd * TP:kd * TP + TP],
                                      start=st, stop=sp),
                             inc="tn" if kd == KD - 1 else None)
                    tn_w[key] = p.cnt["tn"]
                    if not ATTNMONO:
                        ringA.consumed(1, [("tn", p.cnt["tn"])])
                if ATTNMONO:
                    ringA.consumed(3, [("tn", p.cnt["tn"])])
                p.wait("vector", "tn", tn_w["i"])
                if s_ == 0:
                    p.waits("vector", gate["hb_free"] + [("tn", tn_xtr)])
                e1 = p.go("vector", lambda e, s_=s_: e.tensor_scalar(
                    hb[:, s_ * TP:(s_ + 1) * TP], ps_i[:],
                    sv_i[:, s_:s_ + 1], None, ALU.mult), inc="vc")
                bfree["i"] = [("vc", e1)]
                p.wait("scalar", "tn", tn_w["f"])
                if s_ == 0:
                    p.waits("scalar", [("vc", vc_lnf)])
                e2 = p.go("scalar", lambda e, s_=s_: e.activation(
                    zt[:, s_ * TP:(s_ + 1) * TP], ps_f[:], AF.Sigmoid,
                    scale=sv_f[:, s_:s_ + 1]), inc="sc")
                bfree["f"] = [("sc", e2)]
                par = s_ % 2
                p.wait("vector", "tn", tn_w["g"])
                e4 = p.go("vector", lambda e, s_=s_: e.tensor_scalar(
                    gb[:, s_ * TP:(s_ + 1) * TP], ps_g[:],
                    sv_g[:, s_:s_ + 1], None, ALU.mult), inc="vc")
                bfree["g"] = [("vc", e4)]
                p.wait("scalar", "vc", e4)
                p.wait("scalar", "vc", gsg_free[par])
                e3 = p.go("scalar", lambda e, par=par, s_=s_: e.activation(
                    gsg[par][:], gb[:, s_ * TP:(s_ + 1) * TP], AF.Sigmoid),
                    inc="sc")
                p.wait("vector", "sc", e3)
                vc_ifg_done = p.go("vector", lambda e, s_=s_, par=par:
                                   e.tensor_mul(
                                       gb[:, s_ * TP:(s_ + 1) * TP],
                                       gb[:, s_ * TP:(s_ + 1) * TP],
                                       gsg[par][:]), inc="vc")
                gsg_free[par] = vc_ifg_done
                sc_f_done = e2
            p.wait("vector", "sc", sc_f_done)
            for hh in range(2):
                tmp = xna if hh == 0 else xnb
                p.go("vector", lambda e, hh=hh, tmp=tmp: e.tensor_mul(
                    tmp[:], zt[:, hh * D:(hh + 1) * D],
                    hb[:, hh * D:(hh + 1) * D]), inc="vc")
                p.go("vector", lambda e, hh=hh, tmp=tmp: e.tensor_sub(
                    hb[:, hh * D:(hh + 1) * D], hb[:, hh * D:(hh + 1) * D],
                    tmp[:]), inc="vc")
            for s_ in range(KD):
                init = (0.0 if c % (SEQ // TP) == 0 else carry[:, s_:s_ + 1])
                p.go("vector", lambda e, s_=s_, init=init:
                     e.tensor_tensor_scan(
                         hb[:, s_ * TP:(s_ + 1) * TP],
                         zt[:, s_ * TP:(s_ + 1) * TP],
                         hb[:, s_ * TP:(s_ + 1) * TP],
                         init, ALU.mult, ALU.add), inc="vc")
            p.go("vector", lambda e: e.drain())
            for s_ in range(KD):
                vc_carry = p.go("vector", lambda e, s_=s_: e.tensor_copy(
                    carry[:, s_:s_ + 1],
                    hb[:, s_ * TP + TP - 1:s_ * TP + TP]), inc="vc")
            p.wait("scalar", "vc", vc_carry)
            p.wait("scalar", "od", od_prev)
            sq_sc = None
            for i in range(2):
                sq_sc = p.go("scalar", lambda e, i=i: e.activation(
                    (xna if i == 0 else xnb)[:], hb[:, i * D:(i + 1) * D],
                    AF.Square), inc="sc")
            p.wait("tensor", "sc", sq_sc)
            p.waits("tensor", bfree["sb"])
            for kd in range(KD):
                src = xna if kd < 8 else xnb
                tn_rms = p.go("tensor", lambda e, src=src, kd=kd: e.matmul(
                    ps_sb[0:1, 0:256], onc[:],
                    src[:, (kd % 8) * TP:(kd % 8 + 1) * TP],
                    start=(kd == 0), stop=(kd == KD - 1)),
                    inc="tn" if kd == KD - 1 else None)
            p.wait("scalar", "tn", tn_rms)
            sc_sd2 = p.go("scalar", lambda e: e.activation(
                rowb[0:1, 0:256], ps_sb[0:1, 0:256], AF.Sqrt,
                bias=cst[0:1, 1:2], scale=1.0 / D), inc="sc")
            p.wait("vector", "sc", sc_sd2)
            vc_rr = p.go("vector", lambda e: e.reciprocal(
                rowb[0:1, 256:512], rowb[0:1, 0:256]), inc="vc")
            p.wait("tensor", "vc", vc_rr)
            tn_bc = p.go("tensor", lambda e: e.matmul(
                ps_sb[:, 256:512], onr[0:1, :], rowb[0:1, 256:512],
                start=True, stop=True), inc="tn")
            p.wait("vector", "tn", tn_bc)
            vc_bc = p.go("vector", lambda e: e.tensor_copy(
                bc[:, 0:256], ps_sb[:, 256:512]), inc="vc")
            bfree["sb"] = [("vc", vc_bc)]
            p.wait("vector", "sc", sq_sc)
            for s_ in range(KD):
                p.go("vector", lambda e, s_=s_: e.tensor_mul(
                    hb[:, s_ * TP:(s_ + 1) * TP], hb[:, s_ * TP:(s_ + 1) * TP],
                    bc[:, 0:256]), inc="vc")
            p.go("vector", lambda e: e.tensor_mul(hb[:], hb[:], gb[:]),
                 inc="vc")
            for s_ in range(KD):
                p.go("vector", lambda e, s_=s_: e.tensor_scalar(
                    hb[:, s_ * TP:(s_ + 1) * TP], hb[:, s_ * TP:(s_ + 1) * TP],
                    sv_gn[:, s_:s_ + 1], None, ALU.mult), inc="vc")
            p.waits("vector", gate["h2_free"])
            p.go("vector", lambda e: e.tensor_copy(lnh[:, 0:KD * TP], hb[:]), inc="vc")
            vc_o = p.go("vector", lambda e: e.tensor_sub(lnl[:, 0:KD * TP], hb[:], lnh[:, 0:KD * TP]),
                        inc="vc")
            first_wo = True
            for os_ in range(KD):
                slots, wdn = ringB.submit([wos_d[os_, :, :]])
                par = os_ % 2
                wok = "mm0" if par == 0 else "mm1"
                psap = ps_mm[par]
                p.wait("tensor", "wd", wdn)
                if first_wo:
                    p.wait("tensor", "vc", vc_o)
                    first_wo = False
                p.waits("tensor", bfree[wok])
                nmm = 0
                for ke in range(KD):
                    for rhs in (lnh, lnl):
                        last = (ke == KD - 1 and rhs is lnl)
                        tn_wo = p.go("tensor", lambda e, psap=psap, ke=ke,
                                     rhs=rhs, st=(nmm == 0), sp=last,
                                     slot=slots[0]:
                                     e.matmul(psap[:],
                                              slot[:, ke * 128:(ke + 1) * 128],
                                              rhs[:, ke * TP:ke * TP + TP],
                                              start=st, stop=sp),
                                     inc="tn" if last else None)
                        nmm += 1
                ringB.consumed(1, [("tn", p.cnt["tn"])])
                p.wait("scalar", "tn", tn_wo)
                if os_ == 0:
                    p.wait("scalar", "vc", vc_carry)
                e4 = p.go("scalar", lambda e, os_=os_, psap=psap: e.activation(
                    zt[:, os_ * TP:(os_ + 1) * TP], psap[:], AF.Identity,
                    scale=256.0, bias=cst[:, 0:1]), inc="sc")
                bfree[wok] = [("sc", e4)]
            sc_wo = p.cnt["sc"]
            p.wait("vector", "sc", sc_wo)
            p.wait("vector", "od", od_prev)
            p.wait("vector", "tn", p.cnt["tn"])
            for hh in range(2):
                zh = zt[:, hh * D:(hh + 1) * D]
                d_ = xna
                c_ = xnb
                p.go("vector", lambda e, zh=zh: e.tensor_copy(d_[:], zh),
                     inc="vc")
                for b in range(12, -1, -1):
                    p.go("vector", lambda e, b=b: e.tensor_scalar(
                        c_[:], d_[:], float(2 ** b), None, ALU.is_ge), inc="vc")
                    p.go("vector", lambda e, b=b: e.scalar_tensor_tensor(
                        d_[:], c_[:], -float(2 ** b), d_[:], ALU.mult, ALU.add),
                        inc="vc")
                p.go("vector", lambda e, zh=zh: e.scalar_tensor_tensor(
                    zh, d_[:], -1.0, zh, ALU.mult, ALU.add), inc="vc")
                p.go("vector", lambda e, zh=zh: e.tensor_scalar(
                    zh, zh, 1.0 / 256.0, -16.0, ALU.mult, ALU.add), inc="vc")
                vc_y1 = p.go("vector", lambda e, zh=zh, hh=hh: e.tensor_add(
                    xt[:, hh * D:(hh + 1) * D], xt[:, hh * D:(hh + 1) * D],
                    zh), inc="vc")
            p.wait("scalar", "vc", vc_y1)
            sq2 = None
            for i in range(2):
                sq2 = p.go("scalar", lambda e, i=i: e.activation(
                    (xna if i == 0 else xnb)[:], xt[:, i * D:(i + 1) * D],
                    AF.Square), inc="sc")
            p.wait("tensor", "sc", sq2)
            p.waits("tensor", bfree["sb"])
            for kd in range(KD):
                p.go("tensor", lambda e, kd=kd: e.matmul(
                    ps_sb[0:1, 0:256], onc[:], xt[:, kd * TP:(kd + 1) * TP],
                    start=(kd == 0), stop=(kd == KD - 1)),
                    inc="tn" if kd == KD - 1 else None)
            for kd in range(KD):
                src = xna if kd < 8 else xnb
                tn_st2 = p.go("tensor", lambda e, src=src, kd=kd: e.matmul(
                    ps_sb[0:1, 256:512], onc[:],
                    src[:, (kd % 8) * TP:(kd % 8 + 1) * TP],
                    start=(kd == 0), stop=(kd == KD - 1)),
                    inc="tn" if kd == KD - 1 else None)
            p.wait("vector", "tn", tn_st2)
            p.go("vector", lambda e: e.tensor_scalar(
                rowb[0:1, 0:256], ps_sb[0:1, 0:256], 1.0 / D, None, ALU.mult),
                inc="vc")
            p.go("vector", lambda e: e.tensor_scalar(
                rowb[0:1, 256:512], ps_sb[0:1, 256:512], 1.0 / D, None,
                ALU.mult), inc="vc")
            p.go("vector", lambda e: e.drain())
            p.go("vector", lambda e: e.tensor_mul(
                rowb[0:1, 512:768], rowb[0:1, 0:256], rowb[0:1, 0:256]),
                inc="vc")
            p.go("vector", lambda e: e.drain())
            vc_v2 = p.go("vector", lambda e: e.tensor_sub(
                rowb[0:1, 256:512], rowb[0:1, 256:512], rowb[0:1, 512:768]),
                inc="vc")
            p.wait("scalar", "vc", vc_v2)
            sc_sd3 = p.go("scalar", lambda e: e.activation(
                rowb[0:1, 512:768], rowb[0:1, 256:512], AF.Sqrt,
                bias=cst[0:1, 1:2], scale=1.0), inc="sc")
            p.wait("vector", "sc", sc_sd3)
            p.go("vector", lambda e: e.reciprocal(
                rowb[0:1, 256:512], rowb[0:1, 512:768]), inc="vc")
            p.go("vector", lambda e: e.drain())
            vc_mi = p.go("vector", lambda e: e.tensor_mul(
                rowb[0:1, 512:768], rowb[0:1, 0:256], rowb[0:1, 256:512]),
                inc="vc")
            p.wait("tensor", "vc", vc_mi)
            tn_bc2 = p.go("tensor", lambda e: e.matmul(
                ps_sb[:, 0:512], onr[0:1, :], rowb[0:1, 256:768],
                start=True, stop=True), inc="tn")
            p.wait("vector", "tn", tn_bc2)
            vc_bc2 = p.go("vector", lambda e: e.tensor_copy(
                bc[:, 0:512], ps_sb[:, 0:512]), inc="vc")
            bfree["sb"] = [("vc", vc_bc2)]
            p.wait("vector", "tn", p.cnt["tn"])
            for s_ in range(KD):
                p.go("vector", lambda e, s_=s_: e.tensor_mul(
                    zt[:, s_ * TP:(s_ + 1) * TP], xt[:, s_ * TP:(s_ + 1) * TP],
                    bc[:, 0:256]), inc="vc")
            for s_ in range(KD):
                p.go("vector", lambda e, s_=s_: e.tensor_sub(
                    zt[:, s_ * TP:(s_ + 1) * TP], zt[:, s_ * TP:(s_ + 1) * TP],
                    bc[:, 256:512]), inc="vc")
            p.go("vector", lambda e: e.tensor_copy(lnh[:, 0:KD * TP], zt[:]), inc="vc")
            vc_ln2 = p.go("vector", lambda e: e.tensor_sub(
                lnl[:, 0:KD * TP], zt[:], lnh[:, 0:KD * TP]), inc="vc")

            def ffn_stage(n_out, kk, hi_scr, lo_scr, rhs_h, rhs_l, dst_h,
                          dst_l, bias_sb, func, first_wait, h_free_gate,
                          resid=False):
                prev = None
                last_mm = None
                first = True
                sc_last = None
                vc_last = None
                t1_free = [0, 0]
                for s_ in range(n_out):
                    par = s_ % 2
                    fk = "mm0" if par == 0 else "mm1"
                    psap = ps_mm[par]
                    nhalf = kk // KD
                    if False:
                        nmm = 0
                        total = kk * 3
                        if nhalf == 1:
                            srcs = [hi_scr[s_, :, :], lo_scr[s_, :, :]]
                        else:
                            srcs = [hi_scr[s_, :, 0:D], hi_scr[s_, :, D:2 * D],
                                    lo_scr[s_, :, 0:D], lo_scr[s_, :, D:2 * D]]
                        slots, wdn = ringB.submit(srcs)
                        p.wait("tensor", "wd", wdn)
                        if first:
                            p.waits("tensor", first_wait)
                            first = False
                        p.waits("tensor", bfree[fk])

                        def lhs_tile(k):
                            if nhalf == 1:
                                return slots[0][:, k * 128:(k + 1) * 128], \
                                    slots[1][:, k * 128:(k + 1) * 128]
                            if k < 16:
                                return slots[0][:, k * 128:(k + 1) * 128], \
                                    slots[2][:, k * 128:(k + 1) * 128]
                            return slots[1][:, (k - 16) * 128:(k - 15) * 128], \
                                slots[3][:, (k - 16) * 128:(k - 15) * 128]
                        for combo in range(3):
                            for k in range(kk):
                                hi_t, lo_t = lhs_tile(k)
                                lhs = hi_t if combo < 2 else lo_t
                                rhs = rhs_h if combo in (0, 2) else rhs_l
                                last = (nmm == total - 1)
                                last_mm = p.go(
                                    "tensor", lambda e, psap=psap, lhs=lhs,
                                    rhs=rhs, k=k, st=(nmm == 0), sp=last:
                                    e.matmul(psap[:], lhs,
                                             rhs[:, k * TP:k * TP + TP],
                                             start=st, stop=sp),
                                    inc="tn" if last else None)
                                nmm += 1
                        ringB.consumed(len(srcs), [("tn", p.cnt["tn"])])
                    else:
                        nmm = 0
                        total = kk * 3
                        for half in range(nhalf):
                            srcs = [hi_scr[s_, :, half * D:(half + 1) * D],
                                    lo_scr[s_, :, half * D:(half + 1) * D]]
                            slots, wdn = ringB.submit(srcs)
                            p.wait("tensor", "wd", wdn)
                            if first:
                                p.waits("tensor", first_wait)
                                first = False
                            if half == 0:
                                p.waits("tensor", bfree[fk])
                            half_elems = KD * 3
                            for combo in range(3):
                                for kq in range(KD):
                                    k = half * KD + kq
                                    lhs_sl = slots[0] if combo < 2 else slots[1]
                                    lhs = lhs_sl[:, kq * 128:(kq + 1) * 128]
                                    rhs = rhs_h if combo in (0, 2) else rhs_l
                                    last = (nmm == total - 1)
                                    hend = (nmm % half_elems == half_elems - 1)
                                    last_mm = p.go(
                                        "tensor", lambda e, psap=psap, lhs=lhs,
                                        rhs=rhs, k=k, st=(nmm == 0), sp=last:
                                        e.matmul(psap[:], lhs,
                                                 rhs[:, k * TP:k * TP + TP],
                                                 start=st, stop=sp),
                                        inc="tn" if (last or hend) else None)
                                    nmm += 1
                            ringB.consumed(2, [("tn", p.cnt["tn"])])
                    p.wait("scalar", "tn", last_mm)
                    p.wait("scalar", "vc", t1_free[s_ % 2])
                    sc_last = p.go("scalar", lambda e, s_=s_, psap=psap:
                                   e.activation(
                                       t1[s_ % 2][:], psap[:], func,
                                       bias=bias_sb[:, s_:s_ + 1],
                                       scale=1.0), inc="sc")
                    bfree[fk] = [("sc", sc_last)]
                    p.wait("vector", "sc", sc_last)
                    if s_ == 0:
                        p.waits("vector", h_free_gate)
                    if resid:
                        vc_last = p.go("vector", lambda e, s_=s_:
                                       e.tensor_add(
                                           xt[:, s_ * TP:(s_ + 1) * TP],
                                           xt[:, s_ * TP:(s_ + 1) * TP],
                                           t1[s_ % 2][:]), inc="vc")
                        t1_free[s_ % 2] = vc_last
                    else:
                        vc_last = p.go("vector", lambda e, s_=s_: e.tensor_copy(
                            dst_h[:, s_ * TP:(s_ + 1) * TP], t1[s_ % 2][:]),
                            inc="vc")
                        if prev is not None:
                            ps_, buf_ = prev
                            vc_last = p.go(
                                "vector", lambda e, ps_=ps_, buf_=buf_:
                                e.tensor_sub(
                                    dst_l[:, ps_ * TP:(ps_ + 1) * TP],
                                    buf_[:],
                                    dst_h[:, ps_ * TP:(ps_ + 1) * TP]),
                                inc="vc")
                            t1_free[ps_ % 2] = vc_last
                        prev = (s_, t1[s_ % 2])
                    bg_pump(3)
                if not resid and prev is not None:
                    ps_, buf_ = prev
                    p.go("vector", lambda e: e.drain())
                    vc_last = p.go("vector", lambda e, ps_=ps_, buf_=buf_:
                                   e.tensor_sub(
                                       dst_l[:, ps_ * TP:(ps_ + 1) * TP],
                                       buf_[:],
                                       dst_h[:, ps_ * TP:(ps_ + 1) * TP]),
                                   inc="vc")
                return last_mm, sc_last, vc_last

            mm1, _, vc_h1 = ffn_stage(
                KH, KD, w1h_d, w1l_d, lnh, lnl, h1h, h1l, b1s, AF.Relu,
                [("vc", vc_ln2)], gate["h1_free"])
            if c + 1 < nch:
                pro = emit_prologue(c + 1, vc_o, vc_ln2, tn_st2, [])
            mm2, _, vc_h2 = ffn_stage(
                KH, KH, w2h_d, w2l_d, h1h, h1l, h2h, h2l, b2s, AF.Relu,
                [("vc", vc_h1)], gate["h2_free"])
            h1_free_next = [("tn", p.cnt["tn"])]
            mm3, sc3, vc_y2 = ffn_stage(
                KD, KH, w3h_d, w3l_d, h2h, h2l, None, None, b3s, AF.Identity,
                [("vc", vc_h2)], [], resid=True)
            h2_free_next = [("tn", p.cnt["tn"])]
            bg_pump(len(bgq))

            od_last = None
            for i in range(2):
                otiles = [xt[:, os_ * TP + i * 128:os_ * TP + (i + 1) * 128]
                          for os_ in range(KD)]
                dsto = lambda idx, i=i: (xna if i == 0 else xnb)[
                    :, idx * 128:(idx + 1) * 128]
                ev = transpose_pass(otiles, dsto,
                                    [("vc", vc_y2)], [("tn", tn_st2)])
                p.wait("gpsimd", "sc", ev)
                od_last = p.go("gpsimd", lambda e, i=i, r0=r0: e.dma_start(
                    out=y_d[r0 + i * 128:r0 + (i + 1) * 128, :],
                    in_=(xna if i == 0 else xnb)[:]), inc="od", n=16)

            od_prev = od_last
            gate["xt_free"] = [("tn", p.cnt["tn"])]
            gate["hb_free"] = [("vc", p.cnt["vc"])]
            gate["h1_free"] = h1_free_next
            gate["h2_free"] = h2_free_next

        p.wait("gpsimd", "od", p.cnt["od"])
        p.wait("gpsimd", "wd", p.cnt["wd"])

        with nc.Block() as block:
            @block.tensor
            def _(eng):
                for fn in p.streams["tensor"]:
                    fn(eng)

            @block.vector
            def _(eng):
                for fn in p.streams["vector"]:
                    fn(eng)

            @block.scalar
            def _(eng):
                for fn in p.streams["scalar"]:
                    fn(eng)

            @block.gpsimd
            def _(eng):
                for fn in p.streams["gpsimd"]:
                    fn(eng)

            @block.sync
            def _(eng):
                for fn in p.streams["sync"]:
                    fn(eng)
    return nc


def _pack_vec(v, k):
    return np.ascontiguousarray(np.asarray(v, dtype=np.float32)
                                .reshape(k, 128).T)


def _tile_in(w, ks, km):
    w = np.asarray(w, dtype=np.float32)
    return np.ascontiguousarray(
        w.reshape(ks, 128, km, 128).transpose(0, 3, 2, 1)
        .reshape(ks, 128, km * 128))


def _tile_nat(w, ks, km):
    w = np.asarray(w, dtype=np.float32)
    return np.ascontiguousarray(
        w.reshape(km, 128, ks, 128).transpose(2, 1, 0, 3)
        .reshape(ks, 128, km * 128))


def kernel(**inp):
    x = np.asarray(inp["L0_attn_hidden_states"], dtype=np.float32)
    base = {
        "wit": _tile_nat(inp["L0_attn_w_i"], KD, KD),
        "wft": _tile_nat(inp["L0_attn_w_f"], KD, KD),
        "wgt": _tile_nat(inp["L0_attn_w_g"], KD, KD),
        "wot": _tile_in(inp["L0_attn_w_o"], KD, KD),
        "w1t": _tile_in(inp["L0_ffn_W1"], KH, KD),
        "w2t": _tile_in(inp["L0_ffn_W2"], KH, KH),
        "w3t": _tile_in(inp["L0_ffn_W3"], KD, KH),
        "svi": _pack_vec(inp["L0_attn_s_i"], KD),
        "svf": _pack_vec(inp["L0_attn_s_f"], KD),
        "svg": _pack_vec(inp["L0_attn_s_g"], KD),
        "svo": _pack_vec(inp["L0_attn_s_o"], KD),
        "gnw": _pack_vec(inp["L0_attn_g_norm_w"], KD),
        "b1m": _pack_vec(inp["L0_ffn_b1"], KH),
        "b2m": _pack_vec(inp["L0_ffn_b2"], KH),
        "b3m": _pack_vec(inp["L0_ffn_b3"], KD),
        "ident": np.eye(128, dtype=np.float32),
        "onec": np.ones((128, 1), dtype=np.float32),
        "oner": np.ones((1, 128), dtype=np.float32),
    }
    nc = build()
    in_maps = []
    for c in range(8):
        m = dict(base)
        m["x"] = np.ascontiguousarray(x[2 * c:2 * c + 2].reshape(NT, D))
        in_maps.append(m)
    res = run_bass_kernel_spmd(nc, in_maps, core_ids=list(range(8)))
    y = np.concatenate(
        [np.asarray(r["y"]).reshape(2, SEQ, D) for r in res.results], axis=0)
    return y
